# version 07 (final)
# baseline (speedup 1.0000x reference)
import numpy as np
import ml_dtypes
from contextlib import ExitStack

import concourse.bass as bass
import concourse.mybir as mybir
from concourse.bass_utils import run_bass_kernel_spmd

F32 = mybir.dt.float32
BF16 = mybir.dt.bfloat16
AF = mybir.ActivationFunctionType
ALU = mybir.AluOpType
AX = mybir.AxisListType

DM = 1024
NCH = 8
NH = 16
DH = 64
WIN = 256
HALO = 513
SEG = 512
HPG = 2
NHG = NH // HPG
DI = 2730
NDC = 22
QK_SCALE = 8.0
DEPTH = 6
NCORES = 8
PAIRS = [[0, 1], [2, 3], [4, 5], [6, 7]]
MIXED_TT = True



def rotary_tables(positions):
    t = positions.astype(np.float64)
    half = np.arange(0, DH, 2, dtype=np.float64)
    inv_freq = 1.0 / 10000.0 ** (half / DH)
    base_scale = (half + 0.4 * DH) / (1.4 * DH)
    freqs = t[None, :] * inv_freq[:, None]
    power = (t - 4096.0) / (WIN // 2)
    scale = base_scale[:, None] ** power[None, :]
    cos = np.cos(freqs); sin = np.sin(freqs)
    cat = lambda x: np.concatenate([x, x], axis=0)
    A_q = cat(cos * scale); B_q = cat(sin * scale)
    inv = 1.0 / scale
    A_k = cat(cos * inv); B_k = cat(sin * inv)
    return [a.astype(np.float32) for a in (A_q, B_q, A_k, B_k)]


def host_tables(p0, T):
    pos = np.clip(np.arange(p0 - HALO, p0 + T), -600, None)
    A_q, B_q, A_k, B_k = rotary_tables(pos)
    A_q = A_q * QK_SCALE
    B_q = B_q * QK_SCALE
    return [np.concatenate([a, a], axis=0) for a in (A_q, B_q, A_k, B_k)]


def perm_swap_np():
    P = np.zeros((128, 128), dtype=np.float32)
    for b_ in range(2):
        o = b_ * 64
        for j in range(32):
            P[o + j + 32, o + j] = -1.0
            P[o + j, o + j + 32] = 1.0
    return P


def tri_masks_np():
    j = np.arange(128)[:, None]; q = np.arange(128)[None, :]
    return (j >= q).astype(np.float32), (j <= q).astype(np.float32)


def fm_layout(x_td):
    T = x_td.shape[0]
    a = np.ascontiguousarray(x_td.T).reshape(NCH, 128, T).transpose(1, 0, 2)
    return np.ascontiguousarray(a).reshape(128, NCH * T)


def fm_unlayout(a, T):
    a = np.asarray(a).reshape(128, NCH, T).transpose(1, 0, 2).reshape(DM, T)
    return np.ascontiguousarray(a.T)


def col_tile(vec, ncols):
    out = np.zeros((128, ncols), np.float32)
    v = np.asarray(vec, np.float32)
    for c in range(ncols):
        seg = v[c * 128:(c + 1) * 128]
        out[:len(seg), c] = seg
    return out


def col_tile2(vec64):
    return np.tile(np.asarray(vec64, np.float32), 2).reshape(128, 1)


def prep_wqk(Wqkv):
    W = np.asarray(Wqkv).reshape(NCH, 128, 3 * DM)
    out = np.empty((NHG, NCH, 128, 2 * HPG * DH), np.float32)
    for hg in range(NHG):
        q = W[:, :, hg * HPG * DH:(hg + 1) * HPG * DH]
        k = W[:, :, DM + hg * HPG * DH: DM + (hg + 1) * HPG * DH]
        out[hg] = np.concatenate([q, k], axis=-1)
    return np.ascontiguousarray(out)


def prep_wv(Wqkv):
    wv = np.asarray(Wqkv).reshape(NCH, 128, 3 * DM)[:, :, 2 * DM:]
    out = np.zeros((NCH, 128, NH * 65), np.float32)
    for h in range(NH):
        out[:, :, h * 65:h * 65 + 64] = wv[:, :, h * 64:(h + 1) * 64]
    return np.ascontiguousarray(out)


def ones_row_v():
    r = np.zeros((1, 260), np.float32)
    r[0, [64, 129, 194, 259]] = 1.0
    return r


def prep_wchunks(Wmat):
    return np.ascontiguousarray(np.asarray(Wmat).reshape(NCH, 128, -1))


def prep_w1(W1):
    W = np.asarray(W1).reshape(NCH, 128, 2 * DI)
    out = np.zeros((NDC, NCH, 128, 256), np.float32)
    for c in range(NDC):
        c0, c1 = c * 128, min((c + 1) * 128, DI)
        cw = c1 - c0
        out[c, :, :, :cw] = W[:, :, c0:c1]
        out[c, :, :, 128:128 + cw] = W[:, :, DI + c0:DI + c1]
    return np.ascontiguousarray(out)


def prep_w2(W2):
    W2 = np.asarray(W2)
    W2p = np.concatenate([W2, np.zeros((NDC * 128 - DI, DM), W2.dtype)], axis=0)
    return np.ascontiguousarray(W2p.reshape(NDC, 128, DM))



class B:
    def __init__(self, nc):
        self.nc = nc
        self.es = ExitStack()
        self.cnt = {}
        self.sems = {}
        self.block = None

    def __enter__(self):
        self.es.__enter__()
        return self

    def __exit__(self, *a):
        return self.es.__exit__(*a)

    def sem(self, name):
        h = self.es.enter_context(self.nc.semaphore(name))
        self.sems[name] = h
        self.cnt[name] = 0
        return h

    def sb(self, name, shape, dt=F32):
        return self.es.enter_context(self.nc.sbuf_tensor(name, shape, dt)).ap()

    def open(self):
        self.es.enter_context(self.nc.allow_low_precision(
            reason="bf16 operands with f32 accumulation throughout (blessed for this task)"))
        self.block = self.es.enter_context(self.nc.Block())
        for s in ("Lg", "Ls", "P", "S", "V", "G", "C",
                  "wqk0", "wqk1", "wv0", "wv1", "wo0", "wo1", "tab",
                  "w10", "w11", "w20", "w21"):
            self.sem(s)
        self.PS = [self.es.enter_context(
            self.nc.psum_tensor(f"psb{i}", [128, 512], F32)).ap() for i in range(8)]

    def inc(self, inst, sem, n=1):
        inst.then_inc(self.sems[sem], n)
        self.cnt[sem] += n
        return self.cnt[sem]

    def w(self, eng, sem, val=None):
        if val is None:
            val = self.cnt[sem]
        if val > 0:
            eng.wait_ge(self.sems[sem], val)


class Pool:

    def __init__(self, b: B, bf_cols, f32_cols):
        self.pb = b.sb("pool_bf", [128, bf_cols + 2 * f32_cols], BF16)
        self.pf = self.pb.bitcast(F32)
        self.bf_cols = bf_cols + 2 * f32_cols
        self.f32_cols = (bf_cols + 2 * f32_cols) // 2
        self.reset()

    def reset(self):
        self.ob = 0; self.of = 0

    def mark(self):
        return (self.ob, self.of)

    def reset_to(self, m):
        self.ob, self.of = m

    def hi(self):
        return self.ob

    def bf(self, n):
        a = self.pb[:, self.ob:self.ob + n]
        self.ob += n
        assert self.ob <= self.bf_cols, f"bf16 pool overflow {self.ob}"
        return a

    def f32(self, n):
        if self.ob % 2:
            self.ob += 1
        a = self.pf[:, self.ob // 2: self.ob // 2 + n]
        self.ob += 2 * n
        assert self.ob <= self.bf_cols, f"pool overflow {self.ob}"
        return a




def ap3(tile, d1, d2):
    return bass.AP(tile.tensor, tile.offset, [[tile.ap[0][0], 128], [d2, d1], [1, d2]])


def attn_alloc(b: B, pool: Pool, C):
    A = {}
    A["xs"] = pool.bf(NCH * (SEG + 1))
    A["wqk"] = [pool.bf(NCH * 256), pool.bf(NCH * 256)]
    A["wv"] = [pool.bf(NH * 65), pool.bf(NH * 65)]
    A["wo"] = [pool.bf(256), pool.bf(256)]
    A["vt"] = [pool.bf(NH * 65) for _ in range(6)]
    off_cqck = pool.ob
    assert off_cqck % 2 == 0
    A["cq"] = pool.bf(SEG)
    A["ck"] = pool.bf(SEG)
    A["bk"] = [pool.bf(256) for _ in range(NHG)]
    A["pt"] = [pool.bf(768), pool.bf(768)]
    A["attw"] = [pool.bf(NCH * 256), pool.bf(NCH * 256)]
    A["tabA"] = [pool.bf(SEG), pool.bf(SEG)]
    A["tabB"] = [pool.bf(SEG), pool.bf(SEG)]
    A["t_qqs"] = A["pt"][0][:, 0:SEG]
    A["t_kks"] = A["pt"][1][:, 0:SEG]
    A["t_ta"] = pool.bf(SEG); A["t_tb"] = pool.bf(SEG)
    A["t_sum"] = A["t_ta"]
    A["t_sqq"] = A["t_ta"]; A["t_sqk"] = A["t_tb"]
    A["t_rsq"] = A["t_ta"][0:2, :]
    A["t_rsk"] = pool.bf(SEG)[0:2, :]
    ln2 = pool.pf[:, off_cqck // 2: off_cqck // 2 + SEG]
    A["t_lnq"] = ln2[0:2, :]; A["t_lnk"] = ln2[32:34, :]
    A["t_rs"] = pool.bf(1024)
    A["t_rr"] = A["t_rs"]
    A["t_pv"] = pool.bf(1024)
    return A


def gci_of(s0, wo):
    return (s0 + wo + 512) // 128


def emit_attn_depth(b: B, C, d):
    nc = b.nc
    T = C["T"]
    nseg = T // SEG
    A = C["A"]
    passes = [(-1, hg) for hg in range(NHG)] + [(si, hg) for si in range(nseg) for hg in range(NHG)]
    qk_dram = C["qk_dram"][d]

    def load_wqk(i):
        _, hg = passes[i]
        b.w(nc.gpsimd, "P")
        inst = nc.gpsimd.dma_start(out=ap3(A["wqk"][i % 2], NCH, 256),
                                   in_=qk_dram[hg].rearrange("c p n -> p c n"))
        return ("wqk%d" % (i % 2), b.inc(inst, "wqk%d" % (i % 2), 16))

    ready = load_wqk(0)
    for i, (si, hg) in enumerate(passes):
        if hg == 0:
            emit_seg_prologue(b, C, d, si)
        nxt = load_wqk(i + 1) if i + 1 < len(passes) else None
        emit_attn_pass(b, C, d, si, hg, A["wqk"][i % 2], ready)
        ready = nxt


def emit_seg_prologue(b: B, C, d, si):
    nc = b.nc
    A = C["A"]
    T = C["T"]
    halo_seg = si < 0
    s0 = -512 if halo_seg else si * SEG
    x, xh, xs = C["x"], C["xh"], A["xs"]

    toff = HALO + s0
    b.w(nc.gpsimd, "V")
    tl = nc.gpsimd.dma_start(out=A["tabA"][0], in_=C["tabs"][0][:, toff:toff + SEG])
    b.inc(tl, "tab", 16)
    tl = nc.gpsimd.dma_start(out=A["tabB"][0], in_=C["tabs"][1][:, toff:toff + SEG])
    b.inc(tl, "tab", 16)
    tl = nc.gpsimd.dma_start(out=A["tabA"][1], in_=C["tabs"][2][:, toff:toff + SEG])
    b.inc(tl, "tab", 16)
    tl = nc.gpsimd.dma_start(out=A["tabB"][1], in_=C["tabs"][3][:, toff:toff + SEG])
    C["_tab_ready"] = b.inc(tl, "tab", 16)

    if not halo_seg:
        b.w(nc.scalar, "V")
        b.w(nc.scalar, "G")
        b.w(nc.scalar, "Lg")
        for c in range(NCH):
            dst = xs[:, c * (SEG + 1):(c + 1) * (SEG + 1)]
            if s0 == 0:
                nc.scalar.copy(dst[:, 0:1], xh[:, c * HALO + HALO - 1: c * HALO + HALO])
                inst = nc.scalar.copy(dst[:, 1:SEG + 1], x[:, c * T: c * T + SEG])
            else:
                inst = nc.scalar.copy(dst[:, 0:SEG + 1], x[:, c * T + s0 - 1: c * T + s0 + SEG])
        C["_xs_ready"] = b.inc(inst, "S", 1)
    else:
        b.w(nc.scalar, "G")
        C["_xs_ready"] = b.cnt["S"]

    xsrc = xh if halo_seg else xs
    xw = HALO if halo_seg else (SEG + 1)
    wv_dram = C["wv_dram"][d]
    PS = b.PS
    for rnd in range(2):
        ldv = [None] * NCH
        b.w(nc.gpsimd, "P")
        ldv[0] = ("wv0", b.inc(nc.gpsimd.dma_start(out=A["wv"][0], in_=wv_dram[0]), "wv0", 16))
        ldv[1] = ("wv1", b.inc(nc.gpsimd.dma_start(out=A["wv"][1], in_=wv_dram[1]), "wv1", 16))
        b.w(nc.tensor, "S")
        b.w(nc.tensor, "V")
        for c in range(NCH):
            b.w(nc.tensor, *reversed(ldv[c])) if False else b.w(nc.tensor, ldv[c][0], ldv[c][1])
            wvb = A["wv"][c % 2]
            for tl in range(2):
                tc = rnd * 2 + tl
                off = tc * 128 + (1 if c < 4 else 0)
                lhs = xsrc[:, c * xw + off: c * xw + off + 128]
                quad = PS[0:4] if tl == 0 else PS[4:8]
                for q in range(4):
                    mm = nc.tensor.matmul(quad[q][:, 0:260], lhs,
                                          wvb[:, q * 260:(q + 1) * 260],
                                          start=(c == 0), stop=False)
            if c + 2 < NCH:
                p_c = b.inc(mm, "P", 1)
                b.w(nc.gpsimd, "P", p_c)
                sname = "wv%d" % (c % 2)
                ldv[c + 2] = (sname, b.inc(nc.gpsimd.dma_start(out=A["wv"][c % 2],
                                                               in_=wv_dram[c + 2]), sname, 16))
        for tl in range(2):
            tc = rnd * 2 + tl
            quad = PS[0:4] if tl == 0 else PS[4:8]
            for q in range(4):
                mm = nc.tensor.matmul(quad[q][0:128, 0:260], C["ones1_128"], C["onesrow_v"],
                                      start=False, stop=True)
        p_done = b.inc(mm, "P", 1)
        b.w(nc.scalar, "P", p_done)
        for tl in range(2):
            tc = rnd * 2 + tl
            quad = PS[0:4] if tl == 0 else PS[4:8]
            gci = (s0 + 512) // 128 + tc
            vt = A["vt"][gci % 6]
            for q in range(4):
                inst = nc.scalar.copy(vt[:, q * 260:(q + 1) * 260], quad[q][:, 0:260])
        b.inc(inst, "S", 1)


def emit_attn_pass(b: B, C, d, si, hg, wqk, wqk_ready):
    nc = b.nc
    A = C["A"]
    T = C["T"]
    PS = b.PS
    halo_seg = si < 0
    s0 = -512 if halo_seg else si * SEG
    x, xh = C["x"], C["xh"]
    xs = A["xs"]
    ps_q, ps_k, ps_sw, ps_misc = PS[0], PS[1], PS[2], PS[3]
    qs_col, ks_col, negC = C["qs_col"][d], C["ks_col"][d], C["negC"][d]
    xsrc = xh if halo_seg else xs
    xw = HALO if halo_seg else (SEG + 1)

    b.w(nc.tensor, wqk_ready[0], wqk_ready[1])
    b.w(nc.tensor, "S")
    b.w(nc.tensor, "V")
    for c in range(NCH):
        off = 1 if c < 4 else 0
        rhs = xsrc[:, c * xw + off: c * xw + off + SEG]
        lq = wqk[:, c * 256: c * 256 + 128]
        lk = wqk[:, c * 256 + 128: c * 256 + 256]
        nc.tensor.matmul(ps_q, lq, rhs, start=(c == 0), stop=(c == NCH - 1))
        mm = nc.tensor.matmul(ps_k, lk, rhs, start=(c == 0), stop=(c == NCH - 1))
    p_proj = b.inc(mm, "P", 1)

    b.w(nc.scalar, "P", p_proj)
    nc.scalar.activation(A["t_qqs"], ps_q, AF.Copy, scale=qs_col)
    nc.scalar.activation(A["t_kks"], ps_k, AF.Copy, scale=ks_col)
    nc.scalar.activation(A["t_sqq"], ps_q, AF.Square)
    inst = nc.scalar.activation(A["t_sqk"], ps_k, AF.Square)
    s_evac = b.inc(inst, "S", 1)

    b.w(nc.tensor, "S", s_evac)
    nc.tensor.matmul(ps_misc[0:2, :], C["ones2"], A["t_sqq"], start=True, stop=True)
    mm = nc.tensor.matmul(ps_misc[32:34, :], C["ones2"], A["t_sqk"], start=True, stop=True,
                          tile_position=(0, 32))
    p_ss = b.inc(mm, "P", 1)
    mm = nc.tensor.matmul(ps_sw, C["pswap"], A["t_qqs"], start=True, stop=True)
    p_swq = b.inc(mm, "P", 1)

    b.w(nc.scalar, "P", p_ss)
    nc.scalar.activation(A["t_lnq"], ps_misc[0:2, :], AF.Ln, bias=C["eps24"][0:2, :])
    inst = nc.scalar.activation(A["t_lnk"], ps_misc[32:34, :], AF.Ln, bias=C["eps24"][32:34, :])
    s_ln = b.inc(inst, "S", 1)
    b.w(nc.scalar, "S", s_ln)
    nc.scalar.activation(A["t_rsq"], A["t_lnq"], AF.Exp, scale=C["nhalf"][0:2, :])
    inst = nc.scalar.activation(A["t_rsk"], A["t_lnk"], AF.Exp, scale=C["nhalf"][0:2, :])
    s_rstd = b.inc(inst, "S", 1)

    b.w(nc.tensor, "S", s_rstd)
    mm = nc.tensor.matmul(ps_misc, C["onesb2"], A["t_rsq"], start=True, stop=True)
    p_rbq = b.inc(mm, "P", 1)

    cq, ck = A["cq"], A["ck"]
    b.w(nc.vector, "tab", C["_tab_ready"])
    b.w(nc.vector, "P", p_rbq)
    nc.vector.tensor_tensor(A["t_ta"], A["t_qqs"], A["tabA"][0], op=ALU.mult)
    inst = nc.vector.tensor_tensor(A["t_tb"], ps_sw, A["tabB"][0], op=ALU.mult)
    vv = b.inc(inst, "V", 1)
    b.w(nc.vector, "V", vv)
    inst = nc.vector.tensor_tensor(A["t_sum"], A["t_ta"], A["t_tb"], op=ALU.add)
    vv = b.inc(inst, "V", 1)
    b.w(nc.vector, "V", vv)
    for h in range(2):
        r = slice(h * 64, h * 64 + 64)
        inst = nc.vector.tensor_tensor(cq[r, :], A["t_sum"][r, :], ps_misc[r, :], op=ALU.mult)
    v_q = b.inc(inst, "V", 1)

    b.w(nc.tensor, "V", v_q)
    nc.tensor.matmul(ps_sw, C["pswap"], A["t_kks"], start=True, stop=True)
    mm = nc.tensor.matmul(ps_misc, C["onesb2"], A["t_rsk"], start=True, stop=True)
    p_swk = b.inc(mm, "P", 1)

    b.w(nc.vector, "P", p_swk)
    nc.vector.tensor_tensor(A["t_ta"], A["t_kks"], A["tabA"][1], op=ALU.mult)
    inst = nc.vector.tensor_tensor(A["t_tb"], ps_sw, A["tabB"][1], op=ALU.mult)
    vv = b.inc(inst, "V", 1)
    b.w(nc.vector, "V", vv)
    inst = nc.vector.tensor_tensor(A["t_sum"], A["t_ta"], A["t_tb"], op=ALU.add)
    vv = b.inc(inst, "V", 1)
    b.w(nc.vector, "V", vv)
    for h in range(2):
        r = slice(h * 64, h * 64 + 64)
        inst = nc.vector.tensor_tensor(ck[r, :], A["t_sum"][r, :], ps_misc[r, :], op=ALU.mult)
    v_k = b.inc(inst, "V", 1)

    tap = C.get("tap")
    if tap is not None and tap.get("at") == (si, hg, "qk"):
        tap["fn"](b, dict(cq=cq, ck=ck, qqs=A["t_qqs"], kks=A["t_kks"]))

    wins = [] if halo_seg else [(0, A["bk"][hg], si == 0)]
    wins.append((256, None, False))
    pv0, pv1 = PS[6], PS[7]
    simmap = {(0, 0): (PS[4], PS[5]), (0, 1): (PS[0], PS[1]),
              (256, 0): (PS[2], PS[3]), (256, 1): (PS[4], PS[5])}
    for wi, (wo, lbt, isw0) in enumerate(wins):
        lb = lbt if lbt is not None else ck[:, 0:256]
        own = ck[:, wo:wo + 256]
        for h in range(2):
            r = slice(h * 64, h * 64 + 64)
            pa, pb = simmap[(wo, h)]
            b.w(nc.tensor, "V", v_k)
            b.w(nc.tensor, "S")
            nc.tensor.matmul(pa[:, 0:128], lb[r, 0:128], cq[r, wo:wo + 128], start=True, stop=True)
            nc.tensor.matmul(pa[:, 128:256], own[r, 128:256], cq[r, wo + 128:wo + 256],
                             start=True, stop=True)
            nc.tensor.matmul(pa[:, 256:512], lb[r, 128:256], cq[r, wo:wo + 256],
                             start=True, stop=True)
            mm = nc.tensor.matmul(pb[:, 0:256], own[r, 0:128], cq[r, wo:wo + 256],
                                  start=True, stop=True)
            p_sim = b.inc(mm, "P", 1)

            ptile = A["pt"][h]
            b.w(nc.scalar, "P", p_sim)
            if tap is not None:
                b.w(nc.scalar, "G")
            nc.scalar.activation(ptile[:, 0:512], pa, AF.Exp, bias=negC)
            inst = nc.scalar.activation(ptile[:, 512:768], pb[:, 0:256], AF.Exp, bias=negC)
            s_exp = b.inc(inst, "S", 1)

            mU = C["mW0U"] if isw0 else C["mU"]
            b.w(nc.vector, "S", s_exp)
            nc.vector.tensor_tensor(ptile[:, 0:128], ptile[:, 0:128], mU, op=ALU.mult)
            nc.vector.tensor_tensor(ptile[:, 128:256], ptile[:, 128:256], C["mL"], op=ALU.mult)
            if isw0:
                nc.vector.tensor_scalar_mul(ptile[:, 256:384], ptile[:, 256:384], C["mW0F"])
            nc.vector.tensor_tensor(ptile[:, 384:512], ptile[:, 384:512], mU, op=ALU.mult)
            inst = nc.vector.tensor_tensor(ptile[:, 512:640], ptile[:, 512:640], C["mL"], op=ALU.mult)
            v_mask = b.inc(inst, "V", 1)

            H = hg * 2 + h
            pvp = pv0 if h == 0 else pv1
            ucol = wi * 256
            g0 = gci_of(s0, wo)
            vs = lambda g: A["vt"][g % 6][:, H * 65: H * 65 + 65]
            b.w(nc.tensor, "V", v_mask)
            nc.tensor.matmul(pvp[0:65, ucol:ucol + 128], vs(g0 - 2), ptile[:, 0:128],
                             start=True, stop=False)
            nc.tensor.matmul(pvp[0:65, ucol:ucol + 128], vs(g0 - 1), ptile[:, 256:384],
                             start=False, stop=False)
            nc.tensor.matmul(pvp[0:65, ucol:ucol + 128], vs(g0), ptile[:, 512:640],
                             start=False, stop=True)
            nc.tensor.matmul(pvp[0:65, ucol + 128:ucol + 256], vs(g0 + 1), ptile[:, 128:256],
                             start=True, stop=False)
            nc.tensor.matmul(pvp[0:65, ucol + 128:ucol + 256], vs(g0 - 1), ptile[:, 384:512],
                             start=False, stop=False)
            mm = nc.tensor.matmul(pvp[0:65, ucol + 128:ucol + 256], vs(g0), ptile[:, 640:768],
                                  start=False, stop=True)
            p_pv = b.inc(mm, "P", 1)
            if tap is not None and tap.get("at") == (si, hg, "pt", wi, h):
                tap["fn"](b, dict(pt=ptile))
    nwin = len(wins)
    ncols = nwin * 256
    b.w(nc.scalar, "P")
    nc.scalar.activation(A["t_rs"][0:1, 0:ncols], pv0[64:65, 0:ncols], AF.Copy, bias=1e-30)
    inst = nc.scalar.activation(A["t_rs"][0:1, 512:512 + ncols], pv1[64:65, 0:ncols],
                                AF.Copy, bias=1e-30)
    s_rs = b.inc(inst, "S", 1)
    if tap is not None and tap.get("at") == (si, hg, "pv"):
        tap["fn"](b, dict(rs=A["t_rs"]), psums=dict(pv0=pv0, pv1=pv1))
    b.w(nc.vector, "S", s_rs)
    nc.vector.reciprocal(A["t_rr"][0:1, 0:ncols], A["t_rs"][0:1, 0:ncols])
    inst = nc.vector.reciprocal(A["t_rr"][0:1, 512:512 + ncols], A["t_rs"][0:1, 512:512 + ncols])
    v_rr = b.inc(inst, "V", 1)
    b.w(nc.tensor, "V", v_rr)
    b.w(nc.tensor, "S")
    nc.tensor.matmul(PS[4][0:64, 0:ncols], C["ones1_64"], A["t_rr"][0:1, 0:ncols],
                     start=True, stop=True)
    mm = nc.tensor.matmul(PS[5][0:64, 0:ncols], C["ones1_64"], A["t_rr"][0:1, 512:512 + ncols],
                          start=True, stop=True)
    p_rb = b.inc(mm, "P", 1)
    nc.scalar.activation(A["t_pv"][0:64, 0:ncols], pv0[0:64, 0:ncols], AF.Copy)
    inst = nc.scalar.activation(A["t_pv"][0:64, 512:512 + ncols], pv1[0:64, 0:ncols], AF.Copy)
    s_pv = b.inc(inst, "S", 1)
    b.w(nc.vector, "P", p_rb)
    b.w(nc.vector, "S", s_pv)
    for wi in range(nwin):
        attw = A["attw"][wi if not halo_seg else 1]
        u = wi * 256
        nc.vector.tensor_tensor(attw[0:64, hg * 256: hg * 256 + 256],
                                A["t_pv"][0:64, u:u + 256], PS[4][0:64, u:u + 256], op=ALU.mult)
        inst = nc.vector.tensor_tensor(attw[64:128, hg * 256: hg * 256 + 256],
                                       A["t_pv"][0:64, 512 + u:512 + u + 256],
                                       PS[5][0:64, u:u + 256], op=ALU.mult)
    inst = nc.vector.tensor_copy(A["bk"][hg], ck[:, 256:512])
    b.inc(inst, "V", 1)

    if hg == NHG - 1:
        for wi in range(nwin):
            emit_win_wo(b, C, d, s0 + (wo_list := [0, 256])[wi] if not halo_seg else -256,
                        A["attw"][wi if not halo_seg else 1])


def emit_win_wo(b, C, d, t0, attw):
    nc = b.nc
    A = C["A"]
    T = C["T"]
    x, xh = C["x"], C["xh"]
    PS = b.PS
    wo_dram = C["wo_dram"][d]
    v_att = b.cnt["V"]
    nld = 4 * NCH
    ld = [None] * nld
    p_grp = [None] * 4
    b.w(nc.gpsimd, "P")

    def issue_ld(i):
        fp, c = divmod(i, NCH)
        sname = "wo%d" % (i % 2)
        ld[i] = (sname, b.inc(nc.gpsimd.dma_start(
            out=A["wo"][i % 2], in_=wo_dram[c][:, fp * 256:(fp + 1) * 256]), sname, 16))

    issue_ld(0)
    issue_ld(1)

    for fp in range(4):
        pd0, pd1 = PS[0], PS[1]
        b.w(nc.tensor, "V")
        b.w(nc.tensor, "S")
        for c in range(NCH):
            i = fp * NCH + c
            b.w(nc.tensor, ld[i][0], ld[i][1])
            buf = A["wo"][i % 2]
            nc.tensor.matmul(pd0[:, 0:256], buf[:, 0:128], attw[:, c * 256:(c + 1) * 256],
                             start=(c == 0), stop=(c == NCH - 1))
            mm = nc.tensor.matmul(pd1[:, 0:256], buf[:, 128:256], attw[:, c * 256:(c + 1) * 256],
                                  start=(c == 0), stop=(c == NCH - 1))
            p_i = b.inc(mm, "P", 1)
            if i + 2 < nld:
                b.w(nc.gpsimd, "P", p_i)
                issue_ld(i + 2)
        p_grp[fp] = b.cnt["P"]
        b.w(nc.vector, "P", p_grp[fp])
        for j, pd in ((0, pd0), (1, pd1)):
            f = fp * 2 + j
            if t0 < 0:
                dst = xh[:, f * HALO + HALO + t0: f * HALO + HALO + t0 + 256]
            else:
                dst = x[:, f * T + t0: f * T + t0 + 256]
            inst = nc.vector.tensor_tensor(dst, dst, pd[:, 0:256], op=ALU.add)
        b.inc(inst, "V", 1)



def ffn_alloc(b: B, pool: Pool, C):
    F = {}
    region = pool.bf(NCH * (SEG + 1))
    F["xbg"] = region
    F["xn"] = region[:, 0:NCH * 512]
    treg = pool.bf(NDC * 512)
    F["t"] = [treg[:, c * 512:(c + 1) * 512] for c in range(NDC)]
    F["x2g"] = treg[:, 0:NCH * (SEG + 1)]
    F["w1"] = [pool.bf(NCH * 256), pool.bf(NCH * 256)]
    F["w2"] = [pool.bf(1024), pool.bf(1024)]
    F["tg"] = pool.bf(512)
    F["mu_sb"] = [pool.bf(512) for _ in range(4)]
    F["rstd_sb"] = [pool.bf(512) for _ in range(4)]
    F["t1"] = [pool.f32(512), pool.f32(512)]
    F["sm"] = F["t1"][0]
    F["sv"] = F["t1"][1]
    return F


def emit_ffn_depth(b: B, C, d):
    nc = b.nc
    F = C["F"]
    T = C["T"]
    ngr = T // 512
    assert ngr % 4 == 0 or ngr == 2
    nbatch = max(1, ngr // 4)
    bsz = min(4, ngr)
    for bi in reversed(range(nbatch)):
        gs = [bi * bsz + j for j in range(bsz)]
        emit_ffn_stats(b, C, d, gs)
        for g in reversed(gs):
            emit_ffn_group(b, C, d, g, g - gs[0])


def _stage_shifted(b, C, dst, g0, square=False):
    nc = b.nc
    T = C["T"]
    x, xh = C["x"], C["xh"]
    W = SEG + 1
    for c in range(NCH):
        dc = dst[:, c * W:(c + 1) * W]
        if g0 == 0:
            nc.scalar.copy(dc[:, 0:1], xh[:, c * HALO + HALO - 1: c * HALO + HALO])
            inst = nc.scalar.copy(dc[:, 1:W], x[:, c * T: c * T + SEG])
        else:
            inst = nc.scalar.copy(dc[:, 0:W], x[:, c * T + g0 - 1: c * T + g0 + SEG])
    return inst


def emit_ffn_stats(b: B, C, d, gs):
    nc = b.nc
    F = C["F"]
    T = C["T"]
    PS = b.PS
    for gl, g in enumerate(gs):
        g0 = g * 512
        b.w(nc.scalar, "V")
        b.w(nc.scalar, "P")
        b.w(nc.scalar, "Lg")
        inst = _stage_shifted(b, C, F["xbg"], g0)
        s_stage = b.inc(inst, "S", 1)
        b.w(nc.scalar, "S", s_stage)
        for c in range(NCH):
            W = SEG + 1
            inst = nc.scalar.activation(F["x2g"][:, c * W:(c + 1) * W],
                                        F["xbg"][:, c * W:(c + 1) * W], AF.Square)
        s_sq = b.inc(inst, "S", 1)
        mu_ps, m2_ps = PS[0], PS[1]
        b.w(nc.tensor, "S", s_sq)
        b.w(nc.tensor, "V")
        for c in range(NCH):
            W = SEG + 1
            off = 1 if c < 4 else 0
            nc.tensor.matmul(mu_ps, C["jones"], F["xbg"][:, c * W + off: c * W + off + 512],
                             start=(c == 0), stop=(c == NCH - 1))
            mm = nc.tensor.matmul(m2_ps, C["jones"], F["x2g"][:, c * W + off: c * W + off + 512],
                                  start=(c == 0), stop=(c == NCH - 1))
        p_mu = b.inc(mm, "P", 1)
        b.w(nc.scalar, "P", p_mu)
        nc.scalar.activation(F["sm"], mu_ps, AF.Square)
        inst = nc.scalar.copy(F["mu_sb"][gl], mu_ps)
        s_msq = b.inc(inst, "S", 1)
        b.w(nc.vector, "S", s_msq)
        inst = nc.vector.tensor_tensor(F["sv"], m2_ps, F["sm"], op=ALU.subtract)
        v_var = b.inc(inst, "V", 1)
        b.w(nc.scalar, "V", v_var)
        inst = nc.scalar.activation(F["sm"], F["sv"], AF.Ln, bias=C["eps5"][0:128, :])
        s_ln = b.inc(inst, "S", 1)
        b.w(nc.scalar, "S", s_ln)
        inst = nc.scalar.activation(F["rstd_sb"][gl], F["sm"], AF.Exp, scale=C["nhalf"])
        b.inc(inst, "S", 1)


def emit_ffn_group(b: B, C, d, g, gl):
    nc = b.nc
    F = C["F"]
    T = C["T"]
    PS = b.PS
    x, xh = C["x"], C["xh"]
    g0 = g * 512
    gt = C["g_tile"][d]; bt = C["bln_tile"][d]
    b1a = C["b1a_tile"][d]; b1g = C["b1g_tile"][d]
    w1_dram = C["w1_dram"][d]
    w2_dram = C["w2_dram"][d]

    mu, rs = F["mu_sb"][gl], F["rstd_sb"][gl]
    for c in range(NCH):
        t1 = F["t1"][c % 2]
        b.w(nc.vector, "S")
        b.w(nc.vector, "G")
        if c < 4:
            inst = nc.vector.tensor_tensor(t1, x[:, c * T + g0: c * T + g0 + 512], mu,
                                           op=ALU.subtract)
        else:
            if g0 == 0:
                nc.vector.tensor_tensor(t1[:, 0:1], xh[:, c * HALO + HALO - 1: c * HALO + HALO],
                                        mu[:, 0:1], op=ALU.subtract)
                inst = nc.vector.tensor_tensor(t1[:, 1:512], x[:, c * T: c * T + 511],
                                               mu[:, 1:512], op=ALU.subtract)
            else:
                inst = nc.vector.tensor_tensor(t1, x[:, c * T + g0 - 1: c * T + g0 + 511], mu,
                                               op=ALU.subtract)
        v_t1 = b.inc(inst, "V", 1)
        b.w(nc.gpsimd, "V", v_t1)
        inst = nc.gpsimd.tensor_tensor(F["xn"][:, c * 512:(c + 1) * 512], t1, rs, op=ALU.mult)
        g_t2 = b.inc(inst, "G", 1)
        b.w(nc.vector, "G", g_t2)
        inst = nc.vector.tensor_scalar(F["xn"][:, c * 512:(c + 1) * 512],
                                       F["xn"][:, c * 512:(c + 1) * 512],
                                       gt[:, c:c + 1], bt[:, c:c + 1],
                                       op0=ALU.mult, op1=ALU.add)
        b.inc(inst, "V", 1)
    s_xn = b.cnt["V"]
    if g == 0 and C.get("send_hook"):
        C["send_hook"](b, C)

    ld = [None] * NDC

    def load_w1(c):
        sname = "w1%d" % (c % 2)
        b.w(nc.gpsimd, "P")
        ld[c] = (sname, b.inc(nc.gpsimd.dma_start(
            out=ap3(F["w1"][c % 2], NCH, 256),
            in_=w1_dram[c].rearrange("c p n -> p c n")), sname, 16))

    load_w1(0)
    load_w1(1)
    for c in range(NDC):
        pa = PS[2 + 2 * (c % 2)]
        pg = PS[3 + 2 * (c % 2)]
        w1b = F["w1"][c % 2]
        b.w(nc.tensor, ld[c][0], ld[c][1])
        b.w(nc.tensor, "V", s_xn)
        b.w(nc.tensor, "S")
        for cc in range(NCH):
            nc.tensor.matmul(pa, w1b[:, cc * 256: cc * 256 + 128],
                             F["xn"][:, cc * 512:(cc + 1) * 512],
                             start=(cc == 0), stop=(cc == NCH - 1))
            mm = nc.tensor.matmul(pg, w1b[:, cc * 256 + 128: cc * 256 + 256],
                                  F["xn"][:, cc * 512:(cc + 1) * 512],
                                  start=(cc == 0), stop=(cc == NCH - 1))
        p_h = b.inc(mm, "P", 1)
        if c + 2 < NDC:
            load_w1(c + 2)
        b.w(nc.scalar, "P", p_h)
        b.w(nc.scalar, "V")
        inst = nc.scalar.activation(F["tg"], pg, AF.Gelu, bias=b1g[:, c:c + 1])
        s_ev = b.inc(inst, "S", 1)
        b.w(nc.vector, "S", s_ev)
        b.w(nc.vector, "P", p_h)
        inst = nc.vector.scalar_tensor_tensor(F["t"][c], pa, b1a[:, c:c + 1], F["tg"],
                                              op0=ALU.add, op1=ALU.mult)
        b.inc(inst, "V", 1)
    v_t = b.cnt["V"]

    ld2 = [None] * NDC

    def load_w2(c):
        sname = "w2%d" % (c % 2)
        b.w(nc.gpsimd, "P")
        ld2[c] = (sname, b.inc(nc.gpsimd.dma_start(out=F["w2"][c % 2], in_=w2_dram[c]),
                               sname, 16))

    load_w2(0)
    load_w2(1)
    b.w(nc.tensor, "V", v_t)
    b.w(nc.tensor, "S")
    for c in range(NDC):
        w2b = F["w2"][c % 2]
        b.w(nc.tensor, ld2[c][0], ld2[c][1])
        for f in range(NCH):
            mm = nc.tensor.matmul(PS[f], w2b[:, f * 128:(f + 1) * 128], F["t"][c],
                                  start=(c == 0), stop=(c == NDC - 1))
        if c + 2 < NDC:
            p_c = b.inc(mm, "P", 1)
            load_w2(c + 2)
    p_out = b.inc(mm, "P", 1)
    b.w(nc.vector, "P", p_out)
    b2c = C["b2col"][d]
    for f in range(NCH):
        dst = x[:, f * T + g0: f * T + g0 + 512]
        inst = nc.vector.scalar_tensor_tensor(dst, PS[f], b2c[:, f:f + 1], dst,
                                              op0=ALU.add, op1=ALU.add)
    b.inc(inst, "V", 1)



def outer_alloc(b: B, pool: Pool, C):
    O = {}
    O["xgb"] = pool.bf(NCH * 512)
    O["qel"] = pool.bf(NCH * 512)
    O["x2b"] = O["qel"]
    O["sm"] = pool.f32(512)
    O["sv"] = pool.f32(512)
    O["rstd"] = pool.bf(512)
    O["t1"] = [pool.f32(512), pool.f32(512)]
    O["y2"] = pool.bf(NCH * 512)
    O["wbuf"] = [pool.bf(1024), pool.bf(1024)]
    O["ktm"] = pool.bf(1024)
    O["vtm"] = pool.bf(1024)
    O["tmin"] = pool.f32(512)
    O["texp"] = pool.bf(512)
    O["kvf"] = pool.f32(1024)
    O["kvb"] = pool.bf(1024)
    O["aall"] = O["xgb"]
    return O


def emit_ln_group(b: B, C, src_own, src_halo, g0, ntok, dst_own=True):
    nc = b.nc
    O = C["O"]
    T = C["T"]
    x, xh = C["x"], C["xh"]
    PS = b.PS
    b.w(nc.gpsimd, "V"); b.w(nc.gpsimd, "S"); b.w(nc.gpsimd, "Lg"); b.w(nc.gpsimd, "P")
    hw = src_halo.shape[1] // NCH if src_halo is not None else HALO
    if g0 >= 0:
        src = bass.AP(src_own.tensor, src_own.offset + g0,
                      [[src_own.ap[0][0], 128], [T, NCH], [1, ntok]])
    else:
        src = bass.AP(src_halo.tensor, src_halo.offset + hw + g0,
                      [[src_halo.ap[0][0], 128], [hw, NCH], [1, ntok]])
    dst = bass.AP(O["xgb"].tensor, O["xgb"].offset, [[O["xgb"].ap[0][0], 128], [512, NCH], [1, ntok]])
    inst = nc.gpsimd.dma_start(out=dst, in_=src)
    ld = b.inc(inst, "Lg", 16)
    b.w(nc.scalar, "Lg", ld)
    b.w(nc.scalar, "P")
    b.w(nc.scalar, "V")
    b.w(nc.scalar, "G")
    for c in range(NCH):
        inst = nc.scalar.activation(O["x2b"][:, c * 512:c * 512 + ntok],
                                    O["xgb"][:, c * 512:c * 512 + ntok], AF.Square)
    s_q = b.inc(inst, "S", 1)
    mu_ps, m2_ps = PS[0], PS[1]
    b.w(nc.tensor, "S", s_q)
    b.w(nc.tensor, "V")
    for c in range(NCH):
        nc.tensor.matmul(mu_ps[:, 0:ntok], C["jones"], O["xgb"][:, c * 512:c * 512 + ntok],
                         start=(c == 0), stop=(c == NCH - 1))
        mm = nc.tensor.matmul(m2_ps[:, 0:ntok], C["jones"], O["x2b"][:, c * 512:c * 512 + ntok],
                              start=(c == 0), stop=(c == NCH - 1))
    p_mu = b.inc(mm, "P", 1)
    b.w(nc.scalar, "P", p_mu)
    inst = nc.scalar.activation(O["sm"][:, 0:ntok], mu_ps[:, 0:ntok], AF.Square)
    s_m = b.inc(inst, "S", 1)
    b.w(nc.vector, "S", s_m)
    inst = nc.vector.tensor_tensor(O["sv"][:, 0:ntok], m2_ps[:, 0:ntok], O["sm"][:, 0:ntok],
                                   op=ALU.subtract)
    v_v = b.inc(inst, "V", 1)
    b.w(nc.scalar, "V", v_v)
    inst = nc.scalar.activation(O["sm"][:, 0:ntok], O["sv"][:, 0:ntok], AF.Ln, bias=C["eps5"])
    s_l = b.inc(inst, "S", 1)
    b.w(nc.scalar, "S", s_l)
    inst = nc.scalar.activation(O["rstd"][:, 0:ntok], O["sm"][:, 0:ntok], AF.Exp, scale=C["nhalf"])
    s_r = b.inc(inst, "S", 1)
    for c in range(NCH):
        t1 = O["t1"][c % 2]
        b.w(nc.vector, "S", s_r)
        b.w(nc.vector, "G")
        inst = nc.vector.tensor_tensor(t1[:, 0:ntok], O["xgb"][:, c * 512:c * 512 + ntok],
                                       mu_ps[:, 0:ntok], op=ALU.subtract)
        v1 = b.inc(inst, "V", 1)
        b.w(nc.gpsimd, "V", v1)
        if g0 >= 0:
            dstc = x[:, c * T + g0: c * T + g0 + ntok]
        else:
            dstc = xh[:, c * HALO + HALO + g0: c * HALO + HALO + g0 + ntok]
        inst = nc.gpsimd.tensor_tensor(dstc, t1[:, 0:ntok], O["rstd"][:, 0:ntok], op=ALU.mult)
        b.inc(inst, "G", 1)


def emit_outer_ln(b: B, C, src_own, src_halo, do_halo=True):
    nc = b.nc
    T = C["T"]
    if do_halo:
        emit_ln_group(b, C, src_own, src_halo, -513, 257)
        emit_ln_group(b, C, src_own, src_halo, -256, 256)
        if "recvmask" in C:
            b.w(nc.vector, "G")
            inst = nc.vector.tensor_scalar_mul(C["xh"], C["xh"], C["recvmask"])
            b.inc(inst, "V", 1)
    for g0 in range(0, T, 512):
        emit_ln_group(b, C, src_own, src_halo, g0, 512)


def emit_combine(b: B, C, x0_dram, alpha):
    nc = b.nc
    O = C["O"]
    T = C["T"]
    x = C["x"]
    for g0 in range(0, T, 512):
        b.w(nc.gpsimd, "V"); b.w(nc.gpsimd, "S"); b.w(nc.gpsimd, "G"); b.w(nc.gpsimd, "Lg")
        src = bass.AP(x0_dram.tensor, x0_dram.offset + g0,
                      [[x0_dram.ap[0][0], 128], [T, NCH], [1, 512]])
        inst = nc.gpsimd.dma_start(out=O["xgb"], in_=src)
        ld = b.inc(inst, "Lg", 16)
        b.w(nc.vector, "Lg", ld)
        b.w(nc.vector, "P"); b.w(nc.vector, "S"); b.w(nc.vector, "G")
        for c in range(NCH):
            dst = x[:, c * T + g0: c * T + g0 + 512]
            inst = nc.vector.scalar_tensor_tensor(dst, dst, alpha,
                                                  O["xgb"][:, c * 512:(c + 1) * 512],
                                                  op0=ALU.mult, op1=ALU.add)
        b.inc(inst, "V", 1)


def emit_linattn_p1(b: B, C):
    nc = b.nc
    O = C["O"]
    T = C["T"]
    x = C["x"]
    PS = b.PS
    ngr = T // 512
    b.w(nc.vector, "G"); b.w(nc.vector, "S"); b.w(nc.vector, "P")
    inst = nc.vector.memset(O["kvf"][0:64, :], 0.0)
    b.inc(inst, "V", 1)
    for g in range(ngr):
        g0 = g * 512
        b.w(nc.scalar, "P"); b.w(nc.scalar, "V"); b.w(nc.scalar, "G")
        b.w(nc.scalar, "Lg")
        for c in range(NCH):
            inst = nc.scalar.copy(O["xgb"][:, c * 512:(c + 1) * 512], x[:, c * T + g0: c * T + g0 + 512])
        s_c = b.inc(inst, "S", 1)
        b.w(nc.scalar, "S", s_c)
        for c in range(NCH):
            inst = nc.scalar.activation(O["x2b"][:, c * 512:(c + 1) * 512],
                                        O["xgb"][:, c * 512:(c + 1) * 512], AF.Square)
        s_q = b.inc(inst, "S", 1)
        mu_ps, m2_ps = PS[0], PS[1]
        b.w(nc.tensor, "S", s_q)
        b.w(nc.tensor, "V")
        for c in range(NCH):
            nc.tensor.matmul(mu_ps, C["jones"], O["xgb"][:, c * 512:(c + 1) * 512],
                             start=(c == 0), stop=(c == NCH - 1))
            mm = nc.tensor.matmul(m2_ps, C["jones"], O["x2b"][:, c * 512:(c + 1) * 512],
                                  start=(c == 0), stop=(c == NCH - 1))
        p_mu = b.inc(mm, "P", 1)
        b.w(nc.scalar, "P", p_mu)
        inst = nc.scalar.activation(O["sm"], mu_ps, AF.Square)
        s_m = b.inc(inst, "S", 1)
        b.w(nc.vector, "S", s_m)
        inst = nc.vector.tensor_tensor(O["sv"], m2_ps, O["sm"], op=ALU.subtract)
        v_v = b.inc(inst, "V", 1)
        b.w(nc.scalar, "V", v_v)
        inst = nc.scalar.activation(O["sm"], O["sv"], AF.Ln, bias=C["eps5"])
        s_l = b.inc(inst, "S", 1)
        b.w(nc.scalar, "S", s_l)
        inst = nc.scalar.activation(O["rstd"], O["sm"], AF.Exp, scale=C["nhalf"])
        s_r = b.inc(inst, "S", 1)
        for c in range(NCH):
            t1 = O["t1"][c % 2]
            b.w(nc.vector, "S", s_r)
            b.w(nc.vector, "G")
            inst = nc.vector.tensor_tensor(t1, x[:, c * T + g0: c * T + g0 + 512], mu_ps,
                                           op=ALU.subtract)
            v1 = b.inc(inst, "V", 1)
            b.w(nc.gpsimd, "V", v1)
            inst = nc.gpsimd.tensor_tensor(O["y2"][:, c * 512:(c + 1) * 512], t1, O["rstd"],
                                           op=ALU.mult)
            b.inc(inst, "G", 1)
        g_y2 = b.cnt["G"]

        wq_dram = C["wq_dram"]
        ld = [None] * NCH

        def loadw(dram, c, slot):
            sname = "wqk%d" % (c % 2) if False else ("w1%d" % (c % 2))
            b.w(nc.gpsimd, "P")
            return (sname, b.inc(nc.gpsimd.dma_start(out=O["wbuf"][c % 2], in_=dram[c]), sname, 16))

        for half in range(2):
            fs = range(half * 4, half * 4 + 4)
            ld[0] = loadw(wq_dram, 0, 0)
            ld[1] = loadw(wq_dram, 1, 1)
            b.w(nc.tensor, "G", g_y2)
            b.w(nc.tensor, "S")
            b.w(nc.tensor, "V")
            for c in range(NCH):
                b.w(nc.tensor, ld[c][0], ld[c][1])
                for j, f in enumerate(fs):
                    mm = nc.tensor.matmul(PS[j], O["wbuf"][c % 2][:, f * 128:(f + 1) * 128],
                                          O["y2"][:, c * 512:(c + 1) * 512],
                                          start=(c == 0), stop=(c == NCH - 1))
                if c + 2 < NCH:
                    p_c = b.inc(mm, "P", 1)
                    ld[c + 2] = loadw(wq_dram, c + 2, 0)
            p_q = b.inc(mm, "P", 1)
            for j, f in enumerate(fs):
                b.w(nc.vector, "P", p_q)
                b.w(nc.vector, "S")
                b.w(nc.vector, "Lg")
                nc.vector.tensor_scalar(O["tmin"], PS[j], C["bq_tile"][:, f:f + 1], 0.0,
                                        op0=ALU.add, op1=ALU.min)
                inst = nc.vector.tensor_scalar(O["qel"][:, f * 512:(f + 1) * 512], PS[j],
                                               C["bq_tile"][:, f:f + 1], 0.0,
                                               op0=ALU.add, op1=ALU.max)
                v_m = b.inc(inst, "V", 1)
                b.w(nc.scalar, "V", v_m)
                inst = nc.scalar.activation(O["texp"], O["tmin"], AF.Exp)
                s_e = b.inc(inst, "S", 1)
                b.w(nc.vector, "S", s_e)
                inst = nc.vector.tensor_tensor(O["qel"][:, f * 512:(f + 1) * 512],
                                               O["qel"][:, f * 512:(f + 1) * 512], O["texp"],
                                               op=ALU.add)
                b.inc(inst, "V", 1)
        v_q = b.cnt["V"]
        b.w(nc.gpsimd, "V", v_q)
        dq = bass.AP(C["qspill"].tensor, C["qspill"].offset + g0,
                     [[C["qspill"].ap[0][0], 128], [T, NCH], [1, 512]])
        b.inc(nc.gpsimd.dma_start(out=dq, in_=O["qel"]), "Lg", 16)

        wk_dram, wv_dram = C["wk_dram"], C["wvo_dram"]
        for tc in range(4):
            for which, dram, pk0 in (("k", wk_dram, 0), ("v", wv_dram, 2)):
                ldk = [None] * NCH
                ldk[0] = loadw(dram, 0, 0)
                ldk[1] = loadw(dram, 1, 1)
                b.w(nc.tensor, "V")
                b.w(nc.tensor, "S")
                for c in range(NCH):
                    b.w(nc.tensor, ldk[c][0], ldk[c][1])
                    lhs = O["y2"][:, c * 512 + tc * 128: c * 512 + tc * 128 + 128]
                    for hh in range(2):
                        mm = nc.tensor.matmul(PS[pk0 + hh], lhs,
                                              O["wbuf"][c % 2][:, hh * 512:(hh + 1) * 512],
                                              start=(c == 0), stop=False)
                    if c + 2 < NCH:
                        p_c = b.inc(mm, "P", 1)
                        ldk[c + 2] = loadw(dram, c + 2, 0)
                brow = C["bk_row"] if which == "k" else C["bv_row"]
                for hh in range(2):
                    mm = nc.tensor.matmul(PS[pk0 + hh][0:128, :], C["ones1_128"],
                                          brow[0:1, hh * 512:(hh + 1) * 512],
                                          start=False, stop=True)
                p_kv = b.inc(mm, "P", 1)
                dstt = O["ktm"] if which == "k" else O["vtm"]
                if which == "k":
                    for hh in range(2):
                        b.w(nc.vector, "P", p_kv)
                        b.w(nc.vector, "S")
                        nc.vector.tensor_scalar(O["tmin"], PS[pk0 + hh], 0.0, 0.0,
                                                op0=ALU.add, op1=ALU.min)
                        inst = nc.vector.tensor_scalar(dstt[:, hh * 512:(hh + 1) * 512],
                                                       PS[pk0 + hh], 0.0, 0.0,
                                                       op0=ALU.add, op1=ALU.max)
                        v_m = b.inc(inst, "V", 1)
                        b.w(nc.scalar, "V", v_m)
                        inst = nc.scalar.activation(O["texp"], O["tmin"], AF.Exp)
                        s_e = b.inc(inst, "S", 1)
                        b.w(nc.vector, "S", s_e)
                        inst = nc.vector.tensor_tensor(dstt[:, hh * 512:(hh + 1) * 512],
                                                       dstt[:, hh * 512:(hh + 1) * 512],
                                                       O["texp"], op=ALU.add)
                        v_k = b.inc(inst, "V", 1)
                else:
                    b.w(nc.scalar, "P", p_kv)
                    nc.scalar.copy(dstt[:, 0:512], PS[pk0])
                    inst = nc.scalar.copy(dstt[:, 512:1024], PS[pk0 + 1])
                    b.inc(inst, "S", 1)
            pk = (PS[4], PS[5]) if tc % 2 == 0 else (PS[6], PS[7])
            b.w(nc.tensor, "V")
            b.w(nc.tensor, "S")
            for h in range(NH):
                pkv = pk[0] if h < 8 else pk[1]
                col = (h % 8) * 64
                mm = nc.tensor.matmul(pkv[0:64, col:col + 64], O["ktm"][:, h * 64:(h + 1) * 64],
                                      O["vtm"][:, h * 64:(h + 1) * 64],
                                      start=True, stop=True)
            p_kv2 = b.inc(mm, "P", 1)
            b.w(nc.vector, "P", p_kv2)
            nc.vector.tensor_tensor(O["kvf"][0:64, 0:512], O["kvf"][0:64, 0:512],
                                    pk[0][0:64, :], op=ALU.add)
            inst = nc.vector.tensor_tensor(O["kvf"][0:64, 512:1024], O["kvf"][0:64, 512:1024],
                                           pk[1][0:64, :], op=ALU.add)
            b.inc(inst, "V", 1)
    v_kv = b.cnt["V"]
    b.w(nc.gpsimd, "V", v_kv)
    b.inc(nc.gpsimd.dma_start(out=C["kvpart"], in_=O["kvf"][0:64, :]), "Lg", 16)


def emit_linattn_p2(b: B, C, merged=False):
    nc = b.nc
    O = C["O"]
    T = C["T"]
    x = C["x"]
    PS = b.PS
    b.w(nc.gpsimd, "Lg"); b.w(nc.gpsimd, "V"); b.w(nc.gpsimd, "G"); b.w(nc.gpsimd, "S")
    b.w(nc.gpsimd, "C")
    b.inc(nc.gpsimd.dma_start(out=O["kvf"][0:64, :], in_=C["kv_a"]), "Lg", 16)
    b.w(nc.gpsimd, "Lg")
    if merged:
        l2 = b.cnt["Lg"]
    else:
        l2 = b.inc(nc.gpsimd.dma_start(out=O["kvf"][0:64, :], in_=C["kv_b"], accum_op=ALU.add),
                   "Lg", 16)
    b.w(nc.gpsimd, "Lg", l2)
    b.inc(nc.gpsimd.dma_start(out=O["kvb"][0:64, :], in_=O["kvf"][0:64, :]), "Lg", 16)
    l3 = b.inc(nc.gpsimd.dma_start(out=O["kvb"][64:128, :], in_=O["kvf"][0:64, :]), "Lg", 16)
    v_kv = b.cnt["V"]
    wp_dram = C["wp_dram"]
    for g in range(T // 512):
        g0 = g * 512
        b.w(nc.gpsimd, "V")
        sq = bass.AP(C["qspill"].tensor, C["qspill"].offset + g0,
                     [[C["qspill"].ap[0][0], 128], [T, NCH], [1, 512]])
        lq = b.inc(nc.gpsimd.dma_start(out=O["qel"], in_=sq), "Lg", 16)
        b.w(nc.tensor, "Lg", lq)
        b.w(nc.tensor, "S")
        for cch in range(NCH):
            pb = PS[cch % 2]
            b.w(nc.tensor, "S")
            for hh in range(2):
                h = cch * 2 + hh
                r0 = hh * 64
                mm = nc.tensor.matmul(pb[r0:r0 + 64, :],
                                      O["kvb"][r0:r0 + 64, h * 64:(h + 1) * 64],
                                      O["qel"][r0:r0 + 64, cch * 512:(cch + 1) * 512],
                                      start=True, stop=True, tile_position=(r0, r0))
            p_a = b.inc(mm, "P", 1)
            b.w(nc.scalar, "P", p_a)
            inst = nc.scalar.copy(O["aall"][:, cch * 512:(cch + 1) * 512], pb)
            b.inc(inst, "S", 1)
        s_a = b.cnt["S"]
        ldp = [None] * NCH

        def loadwp(c):
            sname = "w2%d" % (c % 2)
            b.w(nc.gpsimd, "P")
            return (sname, b.inc(nc.gpsimd.dma_start(out=O["wbuf"][c % 2], in_=wp_dram[c]),
                                 sname, 16))

        ldp[0] = loadwp(0)
        ldp[1] = loadwp(1)
        b.w(nc.tensor, "S", s_a)
        b.w(nc.tensor, "V")
        for c in range(NCH):
            b.w(nc.tensor, ldp[c][0], ldp[c][1])
            for f in range(NCH):
                mm = nc.tensor.matmul(PS[f], O["wbuf"][c % 2][:, f * 128:(f + 1) * 128],
                                      O["aall"][:, c * 512:(c + 1) * 512],
                                      start=(c == 0), stop=(c == NCH - 1))
            if c + 2 < NCH:
                p_c = b.inc(mm, "P", 1)
                ldp[c + 2] = loadwp(c + 2)
        p_w = b.inc(mm, "P", 1)
        b.w(nc.vector, "P", p_w)
        for f in range(NCH):
            dst = x[:, f * T + g0: f * T + g0 + 512]
            inst = nc.vector.scalar_tensor_tensor(dst, PS[f], C["bp_tile"][:, f:f + 1], dst,
                                                  op0=ALU.add, op1=ALU.add)
        b.inc(inst, "V", 1)



def emit_negC(b: B, C):
    nc = b.nc
    PS = b.PS
    un = lambda v: v[0] if isinstance(v, list) else v
    C = dict(C)
    C["qsrow"] = un(C["qsrow"]); C["ksrow"] = un(C["ksrow"]); C["negC"] = un(C["negC"])
    b.w(nc.vector, "Ls"); b.w(nc.vector, "Lg")
    b.w(nc.vector, "S"); b.w(nc.vector, "P"); b.w(nc.vector, "G")
    inst = nc.vector.tensor_tensor(C["t_qkrow"], C["qsrow"], C["ksrow"], op=ALU.mult)
    v_qk = b.inc(inst, "V", 1)
    b.w(nc.vector, "V", v_qk)
    inst = nc.vector.tensor_reduce(C["t_m1"], C["t_qkrow"], axis=AX.X, op=ALU.max,
                                   apply_absolute_value=True)
    v1 = b.inc(inst, "V", 1)
    b.w(nc.scalar, "V", v1)
    inst = nc.scalar.copy(C["t_m1b"], C["t_m1"])
    s1 = b.inc(inst, "S", 1)
    b.w(nc.tensor, "S", s1)
    mm = nc.tensor.matmul(PS[0][0:128, 0:1], C["ones1_128"], C["t_m1b"], start=True, stop=True)
    p1 = b.inc(mm, "P", 1)
    b.w(nc.vector, "P", p1)
    inst = nc.vector.tensor_scalar(C["t_u"], PS[0][0:128, 0:1], -8.0, 80.0,
                                   op0=ALU.mult, op1=ALU.add)
    v_u = b.inc(inst, "V", 1)
    b.w(nc.vector, "V", v_u)
    inst = nc.vector.tensor_scalar_min(C["negC"], C["t_u"], 30.0)
    b.inc(inst, "V", 1)



TFULL = 4096

ATTN_CONSTS = {
    "ones2": [128, 2], "onesb2": [2, 128], "ones1_64": [1, 64],
    "ones1_128": [1, 128], "onesrow_v": [1, 260],
    "pswap": [128, 128], "mU": [128, 128], "mL": [128, 128],
    "mW0U": [128, 128], "mW0F": [128, 1], "jones": [128, 128],
    "qs_col": [128, 1], "ks_col": [128, 1],
    "eps24": [128, 1], "eps30": [128, 1], "nhalf": [128, 1], "eps5": [128, 1],
    "g_tile": [128, 8], "bln_tile": [128, 8],
    "b1a_tile": [128, NDC], "b1g_tile": [128, NDC], "b2col": [128, 8],
    "qsrow": [1, 64], "ksrow": [1, 64],
}
BF_CONSTS = ("ones2", "onesb2", "ones1_64", "ones1_128", "onesrow_v",
             "pswap", "mU", "mL", "mW0U", "jones")


def build_depth_program():
    nc = bass.Bass("TRN2", target_bir_lowering=False)
    T = TFULL
    x_in = nc.dram_tensor("x_in", [128, NCH * T], F32, kind="ExternalInput").ap()
    xh_in = nc.dram_tensor("xh_in", [128, NCH * HALO], F32, kind="ExternalInput").ap()
    x_out = nc.dram_tensor("x_out", [128, NCH * T], F32, kind="ExternalOutput").ap()
    qk_d = nc.dram_tensor("qk_w", [NHG, NCH, 128, 256], F32, kind="ExternalInput").ap()
    wv_d = nc.dram_tensor("wv_w", [NCH, 128, 1040], F32, kind="ExternalInput").ap()
    wo_d = nc.dram_tensor("wo_w", [NCH, 128, 1024], F32, kind="ExternalInput").ap()
    w1_d = nc.dram_tensor("w1_w", [NDC, NCH, 128, 256], F32, kind="ExternalInput").ap()
    w2_d = nc.dram_tensor("w2_w", [NDC, 128, 1024], F32, kind="ExternalInput").ap()
    tabs_d = [nc.dram_tensor(f"tab{i}", [128, HALO + T], F32, kind="ExternalInput").ap()
              for i in range(4)]
    c_d = {nm: nc.dram_tensor(nm + "_d", shp, F32, kind="ExternalInput").ap()
           for nm, shp in ATTN_CONSTS.items()}

    with B(nc) as b:
        b.open()
        x = b.sb("x", [128, NCH * T], F32)
        xh = b.sb("xh", [128, NCH * HALO], BF16)
        pool = Pool(b, 32500, 1024)
        C = {"T": T, "x": x, "xh": xh, "tabs": tabs_d,
             "qk_dram": [qk_d], "wv_dram": [wv_d], "wo_dram": [wo_d],
             "w1_dram": [w1_d], "w2_dram": [w2_d]}
        m0 = pool.mark()
        C["A"] = attn_alloc(b, pool, C)
        ma = pool.mark()
        pool.reset_to(m0)
        C["F"] = ffn_alloc(b, pool, C)
        mf = pool.mark()
        assert max(ma[0], mf[0]) <= 32500 + 2 * 1024, (ma, mf)
        cs = {nm: b.sb("c_" + nm, shp, BF16 if nm in BF_CONSTS else F32)
              for nm, shp in ATTN_CONSTS.items()}
        C.update(cs)
        for nm in ("qs_col", "ks_col", "g_tile", "bln_tile", "b1a_tile",
                   "b1g_tile", "b2col"):
            C[nm] = [cs[nm]]
        C["negC"] = [b.sb("negC_t", [128, 1], F32)]
        C["t_qkrow"] = b.sb("t_qkrow", [1, 64], F32)
        C["t_m1"] = b.sb("t_m1", [1, 1], F32)
        C["t_m1b"] = b.sb("t_m1b", [1, 1], BF16)
        C["t_u"] = b.sb("t_u", [128, 1], F32)
        nc.const_aps.register(F32, 1e-30, cs["eps30"][0:1, 0:1])

        nc.sync.dma_start(out=x, in_=x_in).then_inc(b.sems["Ls"], 16)
        b.cnt["Ls"] += 16
        b.inc(nc.gpsimd.dma_start(out=xh, in_=xh_in), "Lg", 16)
        for nm in ATTN_CONSTS:
            q = nc.gpsimd if nm in BF_CONSTS else nc.sync
            sem = "Lg" if nm in BF_CONSTS else "Ls"
            b.inc(q.dma_start(out=cs[nm], in_=c_d[nm]), sem, 16)
        for eng in (nc.tensor, nc.vector, nc.scalar, nc.gpsimd):
            b.w(eng, "Ls"); b.w(eng, "Lg")

        negC_t = C["negC"][0]
        C["negC"] = negC_t
        emit_negC(b, C)
        C["negC"] = [negC_t]
        emit_attn_depth(b, C, 0)
        emit_ffn_depth(b, C, 0)

        b.w(nc.sync, "P"); b.w(nc.sync, "V"); b.w(nc.sync, "S"); b.w(nc.sync, "G")
        b.w(nc.sync, "Ls")
        nc.sync.dma_start(out=x_out, in_=x).then_inc(b.sems["Ls"], 16)
        b.cnt["Ls"] += 16
        b.w(nc.sync, "Ls")
        b.w(nc.gpsimd, "Lg")
    return nc


OUTER_CONSTS = {"jones": [128, 128], "ones1_128": [1, 128], "eps5": [128, 1],
                "nhalf": [128, 1], "bq_tile": [128, 8], "bp_tile": [128, 8],
                "bk_row": [1, 1024], "bv_row": [1, 1024]}
OUTER_BF = ("jones", "ones1_128", "bk_row", "bv_row")


def _outer_common(nc, b, pool, C, cshapes, bf_consts):
    cs = {nm: b.sb("c_" + nm, shp, BF16 if nm in bf_consts else F32)
          for nm, shp in cshapes.items()}
    C.update(cs)
    c_d = C["_c_d"]
    for nm in cshapes:
        q = nc.gpsimd if nm in bf_consts else nc.sync
        sem = "Lg" if nm in bf_consts else "Ls"
        b.inc(q.dma_start(out=cs[nm], in_=c_d[nm]), sem, 16)
    for eng in (nc.tensor, nc.vector, nc.scalar, nc.gpsimd):
        b.w(eng, "Ls"); b.w(eng, "Lg")


def build_ln_program():
    nc = bass.Bass("TRN2", target_bir_lowering=False)
    T = TFULL
    x0_d = nc.dram_tensor("x0_in", [128, NCH * T], F32, kind="ExternalInput").ap()
    x0h_d = nc.dram_tensor("x0h_in", [128, NCH * HALO], F32, kind="ExternalInput").ap()
    x_out = nc.dram_tensor("x_out", [128, NCH * T], F32, kind="ExternalOutput").ap()
    xh_out = nc.dram_tensor("xh_out", [128, NCH * HALO], F32, kind="ExternalOutput").ap()
    cshapes = {"jones": [128, 128], "eps5": [128, 1], "nhalf": [128, 1]}
    c_d = {nm: nc.dram_tensor(nm + "_d", shp, F32, kind="ExternalInput").ap()
           for nm, shp in cshapes.items()}
    with B(nc) as b:
        b.open()
        x = b.sb("x", [128, NCH * T], F32)
        xh = b.sb("xh", [128, NCH * HALO], BF16)
        pool = Pool(b, 18600, 7700)
        C = {"T": T, "x": x, "xh": xh, "_c_d": c_d}
        C["O"] = outer_alloc(b, pool, C)
        _outer_common(nc, b, pool, C, cshapes, ("jones",))
        emit_outer_ln(b, C, x0_d, x0h_d)
        b.w(nc.sync, "P"); b.w(nc.sync, "V"); b.w(nc.sync, "S"); b.w(nc.sync, "G")
        b.w(nc.sync, "Ls")
        nc.sync.dma_start(out=x_out, in_=x).then_inc(b.sems["Ls"], 16)
        b.cnt["Ls"] += 16
        b.w(nc.gpsimd, "P"); b.w(nc.gpsimd, "V"); b.w(nc.gpsimd, "S")
        b.inc(nc.gpsimd.dma_start(out=xh_out, in_=xh), "Lg", 16)
        b.w(nc.sync, "Ls"); b.w(nc.gpsimd, "Lg")
    return nc


def build_lin1_program():
    nc = bass.Bass("TRN2", target_bir_lowering=False)
    T = TFULL
    f1_d = nc.dram_tensor("x_in", [128, NCH * T], F32, kind="ExternalInput").ap()
    x0_d = nc.dram_tensor("x0_in", [128, NCH * T], F32, kind="ExternalInput").ap()
    x_out = nc.dram_tensor("x_out", [128, NCH * T], F32, kind="ExternalOutput").ap()
    kv_out = nc.dram_tensor("kv_out", [64, 1024], F32, kind="ExternalOutput").ap()
    q_out = nc.dram_tensor("q_out", [128, NCH * T], BF16, kind="ExternalOutput").ap()
    wq_d = nc.dram_tensor("wq", [NCH, 128, 1024], F32, kind="ExternalInput").ap()
    wk_d = nc.dram_tensor("wk", [NCH, 128, 1024], F32, kind="ExternalInput").ap()
    wv_d = nc.dram_tensor("wv", [NCH, 128, 1024], F32, kind="ExternalInput").ap()
    c_d = {nm: nc.dram_tensor(nm + "_d", shp, F32, kind="ExternalInput").ap()
           for nm, shp in OUTER_CONSTS.items()}
    with B(nc) as b:
        b.open()
        x = b.sb("x", [128, NCH * T], F32)
        xh = b.sb("xh", [128, NCH * HALO], BF16)
        pool = Pool(b, 18600, 7700)
        C = {"T": T, "x": x, "xh": xh, "_c_d": c_d,
             "wq_dram": wq_d, "wk_dram": wk_d, "wvo_dram": wv_d,
             "qspill": q_out, "kvpart": kv_out}
        C["O"] = outer_alloc(b, pool, C)
        _outer_common(nc, b, pool, C, OUTER_CONSTS, OUTER_BF)
        b.w(nc.sync, "Ls")
        nc.sync.dma_start(out=x, in_=f1_d).then_inc(b.sems["Ls"], 16)
        b.cnt["Ls"] += 16
        for eng in (nc.tensor, nc.vector, nc.scalar, nc.gpsimd):
            b.w(eng, "Ls")
        emit_combine(b, C, x0_d, 0.5)
        emit_linattn_p1(b, C)
        b.w(nc.sync, "P"); b.w(nc.sync, "V"); b.w(nc.sync, "S"); b.w(nc.sync, "G")
        b.w(nc.sync, "Ls")
        nc.sync.dma_start(out=x_out, in_=x).then_inc(b.sems["Ls"], 16)
        b.cnt["Ls"] += 16
        b.w(nc.sync, "Ls")
        b.w(nc.gpsimd, "Lg")
    return nc


def build_lin2_program():
    nc = bass.Bass("TRN2", target_bir_lowering=False)
    T = TFULL
    x1_d = nc.dram_tensor("x_in", [128, NCH * T], F32, kind="ExternalInput").ap()
    kva_d = nc.dram_tensor("kv_a", [64, 1024], F32, kind="ExternalInput").ap()
    kvb_d = nc.dram_tensor("kv_b", [64, 1024], F32, kind="ExternalInput").ap()
    q_d = nc.dram_tensor("q_in", [128, NCH * T], BF16, kind="ExternalInput").ap()
    x_out = nc.dram_tensor("x_out", [128, NCH * T], F32, kind="ExternalOutput").ap()
    wp_d = nc.dram_tensor("wp", [NCH, 128, 1024], F32, kind="ExternalInput").ap()
    c_d = {nm: nc.dram_tensor(nm + "_d", shp, F32, kind="ExternalInput").ap()
           for nm, shp in OUTER_CONSTS.items()}
    with B(nc) as b:
        b.open()
        x = b.sb("x", [128, NCH * T], F32)
        xh = b.sb("xh", [128, NCH * HALO], BF16)
        pool = Pool(b, 18600, 7700)
        C = {"T": T, "x": x, "xh": xh, "_c_d": c_d,
             "wp_dram": wp_d, "qspill": q_d, "kv_a": kva_d, "kv_b": kvb_d}
        C["O"] = outer_alloc(b, pool, C)
        _outer_common(nc, b, pool, C, OUTER_CONSTS, OUTER_BF)
        b.w(nc.sync, "Ls")
        nc.sync.dma_start(out=x, in_=x1_d).then_inc(b.sems["Ls"], 16)
        b.cnt["Ls"] += 16
        for eng in (nc.tensor, nc.vector, nc.scalar, nc.gpsimd):
            b.w(eng, "Ls")
        emit_linattn_p2(b, C)
        b.w(nc.sync, "P"); b.w(nc.sync, "V"); b.w(nc.sync, "S"); b.w(nc.sync, "G")
        b.w(nc.sync, "Ls")
        nc.sync.dma_start(out=x_out, in_=x).then_inc(b.sems["Ls"], 16)
        b.cnt["Ls"] += 16
        b.w(nc.sync, "Ls")
        b.w(nc.gpsimd, "Lg")
    return nc


def build_fin_program():
    nc = bass.Bass("TRN2", target_bir_lowering=False)
    T = TFULL
    f2_d = nc.dram_tensor("x_in", [128, NCH * T], F32, kind="ExternalInput").ap()
    x2_d = nc.dram_tensor("x0_in", [128, NCH * T], F32, kind="ExternalInput").ap()
    x_out = nc.dram_tensor("x_out", [128, NCH * T], F32, kind="ExternalOutput").ap()
    with B(nc) as b:
        b.open()
        x = b.sb("x", [128, NCH * T], F32)
        xh = b.sb("xh", [128, NCH * HALO], BF16)
        pool = Pool(b, 4200, 100)
        C = {"T": T, "x": x, "xh": xh}
        C["O"] = {"xgb": pool.bf(NCH * 512)}
        b.w(nc.sync, "Ls")
        nc.sync.dma_start(out=x, in_=f2_d).then_inc(b.sems["Ls"], 16)
        b.cnt["Ls"] += 16
        for eng in (nc.tensor, nc.vector, nc.scalar, nc.gpsimd):
            b.w(eng, "Ls")
        emit_combine(b, C, x2_d, 0.5)
        b.w(nc.sync, "P"); b.w(nc.sync, "V"); b.w(nc.sync, "S"); b.w(nc.sync, "G")
        b.w(nc.sync, "Ls")
        nc.sync.dma_start(out=x_out, in_=x).then_inc(b.sems["Ls"], 16)
        b.cnt["Ls"] += 16
        b.w(nc.sync, "Ls")
    return nc



_PROGS = {}
_DBG = {}


def _prog(name):
    if name not in _PROGS:
        _PROGS[name] = {"depth": build_depth_program, "ln": build_ln_program,
                        "lin1": build_lin1_program, "lin2": build_lin2_program,
                        "fin": build_fin_program}[name]()
    return _PROGS[name]


def _prog_inputs(nc):
    import concourse.mybir as _mb
    names = set()
    for alloc in nc.m.functions[0].allocations:
        if isinstance(alloc, _mb.MemoryLocationSet) and alloc.kind == "ExternalInput":
            names.add(alloc.memorylocations[0].name)
    return names


def _run(name, in_maps):
    nc = _prog(name)
    names = _prog_inputs(nc)
    in_maps = [{k: v for k, v in m.items() if k in names} for m in in_maps]
    res = run_bass_kernel_spmd(nc, in_maps, core_ids=list(range(NCORES)))
    return res.results


def halo_slices(xs_list, c):
    if c % 2 == 0:
        return np.zeros((128, NCH * HALO), np.float32)
    part = xs_list[c - 1]
    out = np.empty((128, NCH * HALO), part.dtype)
    for ch in range(NCH):
        out[:, ch * HALO:(ch + 1) * HALO] = part[:, ch * TFULL + TFULL - HALO: (ch + 1) * TFULL]
    return np.ascontiguousarray(out)


def kernel_unfused(L0_attn_x, L0_attn_Wq, L0_attn_bq, L0_attn_Wk, L0_attn_bk, L0_attn_Wv,
           L0_attn_bv, L0_attn_Wp, L0_attn_bp, L0_ffn_Wqkv, L0_ffn_q_scale,
           L0_ffn_k_scale, L0_ffn_Wo, L0_ffn_ln_g, L0_ffn_ln_b, L0_ffn_W1,
           L0_ffn_b1, L0_ffn_W2, L0_ffn_b2):
    T = TFULL
    x0 = np.asarray(L0_attn_x, np.float32)
    U, L = tri_masks_np()

    x0_own, x0_halo, tabs_by_half = [], [], {}
    for half in (0, 1):
        tabs_by_half[half] = host_tables(half * T, T)
    core_meta = []
    for c in range(NCORES):
        bb, half = c // 2, c % 2
        p0 = half * T
        x0_own.append(fm_layout(x0[bb, p0:p0 + T]))
        if half == 0:
            x0_halo.append(np.zeros((128, NCH * HALO), np.float32))
        else:
            x0_halo.append(fm_layout(x0[bb, p0 - HALO:p0]))
        core_meta.append((bb, half))

    dw = []
    for d in range(DEPTH):
        dw.append({
            "qk_w": prep_wqk(L0_ffn_Wqkv[d]),
            "wv_w": prep_wv(L0_ffn_Wqkv[d]),
            "wo_w": prep_wchunks(L0_ffn_Wo[d]),
            "w1_w": prep_w1(L0_ffn_W1[d]),
            "w2_w": prep_w2(L0_ffn_W2[d]),
            "qs_col_d": col_tile2(L0_ffn_q_scale[d]),
            "ks_col_d": col_tile2(L0_ffn_k_scale[d]),
            "qsrow_d": np.asarray(L0_ffn_q_scale[d], np.float32).reshape(1, 64),
            "ksrow_d": np.asarray(L0_ffn_k_scale[d], np.float32).reshape(1, 64),
            "g_tile_d": col_tile(L0_ffn_ln_g[d], 8),
            "bln_tile_d": col_tile(L0_ffn_ln_b[d], 8),
            "b1a_tile_d": col_tile(np.asarray(L0_ffn_b1[d])[:DI], NDC),
            "b1g_tile_d": col_tile(np.asarray(L0_ffn_b1[d])[DI:], NDC),
            "b2col_d": col_tile(L0_ffn_b2[d], 8),
        })
    base_consts = {
        "ones2_d": np.kron(np.eye(2), np.ones((64, 1))).astype(np.float32),
        "onesb2_d": np.kron(np.eye(2), np.ones((1, 64))).astype(np.float32),
        "ones1_64_d": np.ones((1, 64), np.float32),
        "ones1_128_d": np.ones((1, 128), np.float32),
        "onesrow_v_d": ones_row_v(),
        "pswap_d": perm_swap_np(),
        "mU_d": U, "mL_d": L,
        "jones_d": np.full((128, 128), 1.0 / 1024.0, np.float32),
        "eps24_d": np.full((128, 1), 1e-24, np.float32),
        "eps30_d": np.full((128, 1), 1e-30, np.float32),
        "nhalf_d": np.full((128, 1), -0.5, np.float32),
        "eps5_d": np.full((128, 1), 1e-5, np.float32),
    }
    outer_w = {
        "wq": prep_wchunks(L0_attn_Wq), "wk": prep_wchunks(L0_attn_Wk),
        "wv": prep_wchunks(L0_attn_Wv), "wp": prep_wchunks(L0_attn_Wp),
        "jones_d": base_consts["jones_d"], "ones1_128_d": base_consts["ones1_128_d"],
        "eps5_d": base_consts["eps5_d"], "nhalf_d": base_consts["nhalf_d"],
        "bq_tile_d": col_tile(L0_attn_bq, 8), "bp_tile_d": col_tile(L0_attn_bp, 8),
        "bk_row_d": np.asarray(L0_attn_bk, np.float32).reshape(1, DM),
        "bv_row_d": np.asarray(L0_attn_bv, np.float32).reshape(1, DM),
    }

    def depth_maps(d, xs, xhs):
        maps = []
        for c in range(NCORES):
            bb, half = core_meta[c]
            m = dict(base_consts)
            m.update(dw[d])
            m["x_in"] = xs[c]
            m["xh_in"] = xhs[c]
            for i, t in enumerate(tabs_by_half[half]):
                m[f"tab{i}"] = t
            m["mW0U_d"] = (np.zeros_like(U) if half == 0 else U)
            m["mW0F_d"] = (np.zeros((128, 1), np.float32) if half == 0
                           else np.ones((128, 1), np.float32))
            maps.append(m)
        return maps

    def ffn_module(xs_in, src_own, src_halo):
        ln_maps = [{"x0_in": src_own[c], "x0h_in": src_halo[c],
                    "jones_d": base_consts["jones_d"], "eps5_d": base_consts["eps5_d"],
                    "nhalf_d": base_consts["nhalf_d"]} for c in range(NCORES)]
        r = _run("ln", ln_maps)
        xs = [np.asarray(r[c]["x_out"]) for c in range(NCORES)]
        xhs = [np.asarray(r[c]["xh_out"]) for c in range(NCORES)]
        for d in range(DEPTH):
            if d > 0:
                xhs = [halo_slices(xs, c) for c in range(NCORES)]
            r = _run("depth", depth_maps(d, xs, xhs))
            xs = [np.asarray(r[c]["x_out"]) for c in range(NCORES)]
        return xs

    f1 = ffn_module(None, x0_own, x0_halo)
    lin1_maps = [dict(outer_w, x_in=f1[c], x0_in=x0_own[c],
                      **{k + "": v for k, v in {}.items()}) for c in range(NCORES)]
    r = _run("lin1", lin1_maps)
    x1 = [np.asarray(r[c]["x_out"]) for c in range(NCORES)]
    kvp = [np.asarray(r[c]["kv_out"]) for c in range(NCORES)]
    qsp = [np.asarray(r[c]["q_out"]) for c in range(NCORES)]
    lin2_maps = [dict(outer_w, x_in=x1[c], kv_a=kvp[c], kv_b=kvp[c ^ 1],
                      q_in=qsp[c]) for c in range(NCORES)]
    r = _run("lin2", lin2_maps)
    x2 = [np.asarray(r[c]["x_out"]) for c in range(NCORES)]
    x2_halo = [halo_slices(x2, c) for c in range(NCORES)]
    f2 = ffn_module(None, x2, x2_halo)
    fin_maps = [{"x_in": f2[c], "x0_in": x2[c]} for c in range(NCORES)]
    r = _run("fin", fin_maps)
    outs = [np.asarray(r[c]["x_out"]) for c in range(NCORES)]

    result = np.empty((4, 2 * T, DM), np.float32)
    for c in range(NCORES):
        bb, half = core_meta[c]
        result[bb, half * T:(half + 1) * T] = fm_unlayout(outs[c], T)
    return result



def emit_halo_send(b: B, C, tok0=None):
    nc = b.nc
    T = C["T"]
    if tok0 is None:
        tok0 = T - HALO
    x = C["x"]
    stage = C["xh"]
    b.w(nc.vector, "Lg")
    b.w(nc.vector, "P"); b.w(nc.vector, "S")
    b.w(nc.vector, "V")
    for c in range(NCH):
        inst = nc.vector.tensor_scalar_mul(
            stage[:, c * HALO: c * HALO + HALO],
            x[:, c * T + tok0: c * T + tok0 + HALO], C["sendmask"])
    v_st = b.inc(inst, "V", 1)
    b.w(nc.gpsimd, "V", v_st)
    b.w(nc.gpsimd, "C")
    dst = C["hsend"]
    src3 = bass.AP(stage.tensor, stage.offset, [[stage.ap[0][0], 128], [HALO, NCH], [1, HALO]])
    ld = b.inc(nc.gpsimd.dma_start(out=dst, in_=src3), "Lg", 16)
    b.w(nc.gpsimd, "Lg", ld)
    inst = nc.gpsimd.collective_compute(
        "AllReduce", ALU.add, replica_groups=PAIRS,
        ins=[C["hsend_t"].ap().opt()], outs=[C["hrecv_t"].ap().opt()])
    b.inc(inst, "C", 1)


def emit_halo_recv(b: B, C):
    nc = b.nc
    xh = C["xh"]
    b.w(nc.gpsimd, "C")
    b.w(nc.gpsimd, "V"); b.w(nc.gpsimd, "S"); b.w(nc.gpsimd, "P")
    ld = b.inc(nc.gpsimd.dma_start(out=xh, in_=C["hrecv"]), "Lg", 16)
    b.w(nc.vector, "Lg", ld)
    inst = nc.vector.tensor_scalar_mul(xh, xh, C["recvmask"])
    b.inc(inst, "V", 1)


def build_fused_program(stages="full", T=TFULL):
    nc = bass.Bass("TRN2", target_bir_lowering=False)
    x0_in = nc.dram_tensor("x0_in", [128, NCH * T], F32, kind="ExternalInput").ap()
    x0h_in = nc.dram_tensor("x0h_in", [128, NCH * HALO], F32, kind="ExternalInput").ap()
    x_out = nc.dram_tensor("x_out", [128, NCH * T], F32, kind="ExternalOutput").ap()
    qk_d = nc.dram_tensor("qk_w", [DEPTH, NHG, NCH, 128, 256], F32, kind="ExternalInput").ap()
    wv_d = nc.dram_tensor("wv_w", [DEPTH, NCH, 128, 1040], F32, kind="ExternalInput").ap()
    wo_d = nc.dram_tensor("wo_w", [DEPTH, NCH, 128, 1024], F32, kind="ExternalInput").ap()
    w1_d = nc.dram_tensor("w1_w", [DEPTH, NDC, NCH, 128, 256], F32, kind="ExternalInput").ap()
    w2_d = nc.dram_tensor("w2_w", [DEPTH, NDC, 128, 1024], F32, kind="ExternalInput").ap()
    tabs_d = [nc.dram_tensor(f"tab{i}", [128, HALO + T], F32, kind="ExternalInput").ap()
              for i in range(4)]
    PAR = {"qs_col": [128, 1], "ks_col": [128, 1], "g_tile": [128, 8],
           "bln_tile": [128, 8], "b1a_tile": [128, NDC], "b1g_tile": [128, NDC],
           "b2col": [128, 8], "qsrow": [1, 64], "ksrow": [1, 64]}
    par_d = {nm: nc.dram_tensor(nm + "_d", [DEPTH] + shp, F32, kind="ExternalInput").ap()
             for nm, shp in PAR.items()}
    GC = {"ones2": [128, 2], "onesb2": [2, 128], "ones1_64": [1, 64],
          "ones1_128": [1, 128], "onesrow_v": [1, 260],
          "pswap": [128, 128], "mU": [128, 128], "mL": [128, 128],
          "mW0U": [128, 128], "mW0F": [128, 1], "jones": [128, 128],
          "eps24": [128, 1], "eps30": [128, 1], "nhalf": [128, 1], "eps5": [128, 1],
          "bq_tile": [128, 8], "bp_tile": [128, 8],
          "bk_row": [1, 1024], "bv_row": [1, 1024],
          "sendmask": [128, 1], "recvmask": [128, 1]}
    GBF = ("ones2", "onesb2", "ones1_64", "ones1_128", "onesrow_v",
           "pswap", "mU", "mL", "mW0U", "jones", "bk_row", "bv_row")
    gc_d = {nm: nc.dram_tensor(nm + "_d", shp, F32, kind="ExternalInput").ap()
            for nm, shp in GC.items()}
    wq_d = nc.dram_tensor("wq", [NCH, 128, 1024], F32, kind="ExternalInput").ap()
    wk_d = nc.dram_tensor("wk", [NCH, 128, 1024], F32, kind="ExternalInput").ap()
    wvo_d = nc.dram_tensor("wvo", [NCH, 128, 1024], F32, kind="ExternalInput").ap()
    wp_d = nc.dram_tensor("wp", [NCH, 128, 1024], F32, kind="ExternalInput").ap()
    x2_sp = nc.dram_tensor("x2_sp", [128, NCH * T], F32).ap()
    qspill = nc.dram_tensor("qspill", [128, NCH * T], BF16).ap()
    kvpart_t = nc.dram_tensor("kvpart", [64, 1024], F32)
    kvsum_t = nc.dram_tensor("kvsum", [64, 1024], F32)
    hsend_t = nc.dram_tensor("hsend", [128, NCH * HALO], BF16)
    hrecv_t = nc.dram_tensor("hrecv", [128, NCH * HALO], BF16)

    with B(nc) as b:
        b.open()
        x = b.sb("x", [128, NCH * T], F32)
        xh = b.sb("xh", [128, NCH * HALO], BF16)
        pool = Pool(b, 31368, 0)
        C = {"T": T, "x": x, "xh": xh, "tabs": tabs_d,
             "qk_dram": [qk_d[d] for d in range(DEPTH)],
             "wv_dram": [wv_d[d] for d in range(DEPTH)],
             "wo_dram": [wo_d[d] for d in range(DEPTH)],
             "w1_dram": [w1_d[d] for d in range(DEPTH)],
             "w2_dram": [w2_d[d] for d in range(DEPTH)],
             "wq_dram": wq_d, "wk_dram": wk_d, "wvo_dram": wvo_d, "wp_dram": wp_d,
             "qspill": qspill, "kvpart": kvpart_t.ap(), "kv_a": kvsum_t.ap(), "kv_b": None,
             "hsend": hsend_t.ap(), "hrecv": hrecv_t.ap(),
             "hsend_t": hsend_t, "hrecv_t": hrecv_t}
        m0 = pool.mark()
        C["A"] = attn_alloc(b, pool, C)
        ma = pool.mark()
        pool.reset_to(m0)
        C["F"] = ffn_alloc(b, pool, C)
        mf = pool.mark()
        pool.reset_to(m0)
        C["O"] = outer_alloc(b, pool, C)
        mo = pool.mark()
        assert max(ma[0], mf[0], mo[0]) <= 31368, (ma, mf, mo)
        cs = {nm: b.sb("c_" + nm, shp, BF16 if nm in GBF else F32)
              for nm, shp in GC.items()}
        C.update(cs)
        pt = {nm: b.sb("p_" + nm, shp, F32) for nm, shp in PAR.items()}
        for nm in PAR:
            C[nm] = [pt[nm]] * DEPTH
        C["negC"] = [b.sb("negC_t", [128, 1], F32)] * DEPTH
        C["t_qkrow"] = b.sb("t_qkrow", [1, 64], F32)
        C["t_m1"] = b.sb("t_m1", [1, 1], F32)
        C["t_m1b"] = b.sb("t_m1b", [1, 1], BF16)
        C["t_u"] = b.sb("t_u", [128, 1], F32)
        nc.const_aps.register(F32, 1e-30, cs["eps30"][0:1, 0:1])

        for nm in GC:
            q = nc.gpsimd if nm in GBF else nc.sync
            sem = "Lg" if nm in GBF else "Ls"
            b.inc(q.dma_start(out=cs[nm], in_=gc_d[nm]), sem, 16)
        for eng in (nc.tensor, nc.vector, nc.scalar, nc.gpsimd):
            b.w(eng, "Ls"); b.w(eng, "Lg")

        def load_params(d):
            b.w(nc.sync, "Ls")
            b.w(nc.sync, "P"); b.w(nc.sync, "V"); b.w(nc.sync, "S")
            b.w(nc.sync, "G")
            for nm in PAR:
                inst = nc.sync.dma_start(out=pt[nm], in_=par_d[nm][d])
                b.inc(inst, "Ls", 16)
            lv = b.cnt["Ls"]
            for eng in (nc.tensor, nc.vector, nc.scalar, nc.gpsimd):
                b.w(eng, "Ls", lv)

        base = stages[:-1] if stages.endswith("x") else stages
        hooks_on = not stages.endswith("x")
        ndep = {"ln": 0, "att1": 1, "attld": 1, "d1": 1, "d2": 2,
                "ffn1": DEPTH}.get(base, DEPTH)
        dbg_negc = nc.dram_tensor("dbg_negc", [128, 2], F32, kind="ExternalOutput").ap()
        dbg_xh = nc.dram_tensor("dbg_xh", [128, NCH * HALO], F32, kind="ExternalOutput").ap()
        TAPS = {"xs": NCH * (SEG + 1), "wqk": NCH * 256, "vt4": NH * 65,
                "vt5": NH * 65, "cq": SEG, "ck": SEG}
        dbg_tap = {nm: nc.dram_tensor("dbgt_" + nm, [128, w], BF16,
                                      kind="ExternalOutput").ap()
                   for nm, w in TAPS.items()}

        def ffn_module(src_own, src_halo):
            if base == "attld":
                b.w(nc.sync, "Ls")
                nc.sync.dma_start(out=x, in_=src_own).then_inc(b.sems["Ls"], 16)
                b.cnt["Ls"] += 16
                b.w(nc.gpsimd, "Lg"); b.w(nc.gpsimd, "V"); b.w(nc.gpsimd, "S")
                b.inc(nc.gpsimd.dma_start(out=xh, in_=src_halo), "Lg", 16)
                lv0, lg0 = b.cnt["Ls"], b.cnt["Lg"]
                for eng in (nc.tensor, nc.vector, nc.scalar, nc.gpsimd):
                    b.w(eng, "Ls", lv0); b.w(eng, "Lg", lg0)
            else:
                emit_outer_ln(b, C, src_own, src_halo)

            if base == "attld":
                def tapfn(bb, tiles, psums=None):
                    A = C["A"]
                    srcs = {"xs": A["xs"], "wqk": A["wqk"][0], "vt4": A["vt"][4],
                            "vt5": A["vt"][5], "cq": tiles["cq"], "ck": tiles["ck"]}
                    bb.w(nc.gpsimd, "V"); bb.w(nc.gpsimd, "S"); bb.w(nc.gpsimd, "P")
                    for nm, ap in srcs.items():
                        bb.inc(nc.gpsimd.dma_start(out=dbg_tap[nm], in_=ap), "Lg", 16)
                    bb.w(nc.gpsimd, "Lg")
                    bb.w(nc.vector, "Lg"); bb.w(nc.scalar, "Lg")
                    bb.w(nc.tensor, "Lg")
                C["tap"] = {"at": (0, 0, "qk"), "fn": tapfn}
            for d in range(ndep):
                load_params(d)
                negt = C["negC"][0]
                C["negC"] = negt
                emit_negC(b, C)
                C["negC"] = [negt] * DEPTH
                if d > 0 and hooks_on:
                    emit_halo_recv(b, C)
                C["send_hook"] = (emit_halo_send if (d < DEPTH - 1 and hooks_on)
                                  else None)
                emit_attn_depth(b, C, d)
                if base in ("att1", "attld"):
                    break
                emit_ffn_depth(b, C, d)

        ffn_module(x0_in, x0h_in)
        if base in ("ln", "att1", "attld", "d1", "d2", "ffn1"):
            b.w(nc.sync, "P"); b.w(nc.sync, "V"); b.w(nc.sync, "S"); b.w(nc.sync, "G")
            b.w(nc.sync, "Ls")
            nc.sync.dma_start(out=x_out, in_=x).then_inc(b.sems["Ls"], 16)
            b.cnt["Ls"] += 16
            if base in ("att1", "attld"):
                nt = C["negC"][0] if isinstance(C["negC"], list) else C["negC"]
                with nc.allow_non_contiguous_dma(reason="debug dump"):
                    nc.sync.dma_start(out=dbg_negc[:, 0:1], in_=nt).then_inc(b.sems["Ls"], 16)
                b.cnt["Ls"] += 16
                b.w(nc.gpsimd, "V"); b.w(nc.gpsimd, "S"); b.w(nc.gpsimd, "P")
                b.inc(nc.gpsimd.dma_start(out=dbg_xh, in_=C["xh"]), "Lg", 16)
            b.w(nc.sync, "Ls")
            b.w(nc.gpsimd, "Lg")
            b.w(nc.gpsimd, "C")
            return nc
        emit_combine(b, C, x0_in, 0.5)
        emit_linattn_p1(b, C)
        b.w(nc.gpsimd, "Lg")
        b.w(nc.gpsimd, "C")
        inst = nc.gpsimd.collective_compute(
            "AllReduce", ALU.add, replica_groups=PAIRS,
            ins=[kvpart_t.ap().opt()], outs=[kvsum_t.ap().opt()])
        b.inc(inst, "C", 1)
        emit_linattn_p2(b, C, merged=True)
        b.w(nc.sync, "P"); b.w(nc.sync, "V"); b.w(nc.sync, "S"); b.w(nc.sync, "Ls")
        sp = b.inc(nc.sync.dma_start(out=x2_sp, in_=x), "Ls", 16)
        emit_halo_send(b, C)
        b.w(nc.gpsimd, "C")
        for eng in (nc.tensor, nc.vector, nc.scalar, nc.gpsimd):
            b.w(eng, "Ls", sp)
        ffn_module(x2_sp, C["hrecv"])
        emit_combine(b, C, x2_sp, 0.5)
        b.w(nc.sync, "P"); b.w(nc.sync, "V"); b.w(nc.sync, "S"); b.w(nc.sync, "G")
        b.w(nc.sync, "Ls")
        nc.sync.dma_start(out=x_out, in_=x).then_inc(b.sems["Ls"], 16)
        b.cnt["Ls"] += 16
        b.w(nc.sync, "Ls")
        b.w(nc.gpsimd, "Lg")
        b.w(nc.gpsimd, "C")
    return nc


def kernel(L0_attn_x, L0_attn_Wq, L0_attn_bq, L0_attn_Wk, L0_attn_bk, L0_attn_Wv,
           L0_attn_bv, L0_attn_Wp, L0_attn_bp, L0_ffn_Wqkv, L0_ffn_q_scale,
           L0_ffn_k_scale, L0_ffn_Wo, L0_ffn_ln_g, L0_ffn_ln_b, L0_ffn_W1,
           L0_ffn_b1, L0_ffn_W2, L0_ffn_b2):
    T = TFULL
    x0 = np.asarray(L0_attn_x, np.float32)
    U, L = tri_masks_np()

    stacked = {
        "qk_w": np.stack([prep_wqk(L0_ffn_Wqkv[d]) for d in range(DEPTH)]),
        "wv_w": np.stack([prep_wv(L0_ffn_Wqkv[d]) for d in range(DEPTH)]),
        "wo_w": np.stack([prep_wchunks(L0_ffn_Wo[d]) for d in range(DEPTH)]),
        "w1_w": np.stack([prep_w1(L0_ffn_W1[d]) for d in range(DEPTH)]),
        "w2_w": np.stack([prep_w2(L0_ffn_W2[d]) for d in range(DEPTH)]),
        "qs_col_d": np.stack([col_tile2(L0_ffn_q_scale[d]) for d in range(DEPTH)]),
        "ks_col_d": np.stack([col_tile2(L0_ffn_k_scale[d]) for d in range(DEPTH)]),
        "qsrow_d": np.stack([np.asarray(L0_ffn_q_scale[d], np.float32).reshape(1, 64)
                             for d in range(DEPTH)]),
        "ksrow_d": np.stack([np.asarray(L0_ffn_k_scale[d], np.float32).reshape(1, 64)
                             for d in range(DEPTH)]),
        "g_tile_d": np.stack([col_tile(L0_ffn_ln_g[d], 8) for d in range(DEPTH)]),
        "bln_tile_d": np.stack([col_tile(L0_ffn_ln_b[d], 8) for d in range(DEPTH)]),
        "b1a_tile_d": np.stack([col_tile(np.asarray(L0_ffn_b1[d])[:DI], NDC)
                                for d in range(DEPTH)]),
        "b1g_tile_d": np.stack([col_tile(np.asarray(L0_ffn_b1[d])[DI:], NDC)
                                for d in range(DEPTH)]),
        "b2col_d": np.stack([col_tile(L0_ffn_b2[d], 8) for d in range(DEPTH)]),
    }
    gc = {
        "ones2_d": np.kron(np.eye(2), np.ones((64, 1))).astype(np.float32),
        "onesb2_d": np.kron(np.eye(2), np.ones((1, 64))).astype(np.float32),
        "ones1_64_d": np.ones((1, 64), np.float32),
        "ones1_128_d": np.ones((1, 128), np.float32),
        "onesrow_v_d": ones_row_v(),
        "pswap_d": perm_swap_np(),
        "mU_d": U, "mL_d": L,
        "jones_d": np.full((128, 128), 1.0 / 1024.0, np.float32),
        "eps24_d": np.full((128, 1), 1e-24, np.float32),
        "eps30_d": np.full((128, 1), 1e-30, np.float32),
        "nhalf_d": np.full((128, 1), -0.5, np.float32),
        "eps5_d": np.full((128, 1), 1e-5, np.float32),
        "bq_tile_d": col_tile(L0_attn_bq, 8), "bp_tile_d": col_tile(L0_attn_bp, 8),
        "bk_row_d": np.asarray(L0_attn_bk, np.float32).reshape(1, DM),
        "bv_row_d": np.asarray(L0_attn_bv, np.float32).reshape(1, DM),
        "wq": prep_wchunks(L0_attn_Wq), "wk": prep_wchunks(L0_attn_Wk),
        "wvo": prep_wchunks(L0_attn_Wv), "wp": prep_wchunks(L0_attn_Wp),
    }
    tabs_by_half = {h: host_tables(h * T, T) for h in (0, 1)}

    in_maps = []
    for c in range(NCORES):
        bb, half = c // 2, c % 2
        p0 = half * T
        m = dict(gc)
        m.update(stacked)
        m["x0_in"] = fm_layout(x0[bb, p0:p0 + T])
        m["x0h_in"] = (np.zeros((128, NCH * HALO), np.float32) if half == 0
                       else fm_layout(x0[bb, p0 - HALO:p0]))
        for i, t in enumerate(tabs_by_half[half]):
            m[f"tab{i}"] = t
        m["mW0U_d"] = (np.zeros_like(U) if half == 0 else U)
        m["mW0F_d"] = (np.zeros((128, 1), np.float32) if half == 0
                       else np.ones((128, 1), np.float32))
        m["sendmask_d"] = np.full((128, 1), 1.0 if half == 0 else 0.0, np.float32)
        m["recvmask_d"] = np.full((128, 1), 0.0 if half == 0 else 1.0, np.float32)
        in_maps.append(m)

    if "fused" not in _PROGS:
        _PROGS["fused"] = build_fused_program()
    nc = _PROGS["fused"]
    names = _prog_inputs(nc)
    in_maps = [{k: v for k, v in m.items() if k in names} for m in in_maps]
    res = run_bass_kernel_spmd(nc, in_maps, core_ids=list(range(NCORES)))
    _DBG.clear()
    for nm in list(res.results[0].keys()):
        if not nm.startswith("dbgt_"):
            continue
        _DBG[nm] = np.stack([np.asarray(res.results[c][nm]) for c in range(NCORES)])
    for nm in ("dbg_negc", "dbg_xh"):
        if nm in res.results[0]:
            _DBG[nm] = np.stack([np.asarray(res.results[c][nm])
                                 for c in range(NCORES)])
    result = np.empty((4, 2 * T, DM), np.float32)
    for c in range(NCORES):
        bb, half = c // 2, c % 2
        result[bb, half * T:(half + 1) * T] = fm_unlayout(
            np.asarray(res.results[c]["x_out"]), T)
    return result
